# version 21
# baseline (speedup 1.0000x reference)
import numpy as np
from contextlib import ExitStack

import concourse.bass as bass
import concourse.mybir as mybir
from concourse.bass_utils import run_bass_kernel_spmd

F32 = mybir.dt.float32
BF16 = mybir.dt.bfloat16
AF = mybir.ActivationFunctionType
ALU = mybir.AluOpType
AX = mybir.AxisListType

NCORE = 8
ROWS = 2048
T = 512
NMEG = ROWS // T
NSUB = 4
SUB = 128
HID = 2048
NCH = 16
H = 16
NB = 4
E8 = 8
EPS_LN = 1e-5
EPS_A = 1e-6
SCALING = 1.0 / 64.0

_TAPS = False
_DMA_SERIAL = False


def build_nc():
    nc = bass.Bass("TRN2", target_bir_lowering=False)
    taps = {}
    if _TAPS:
        for name, shape, dt in [
            ("tap_stats", [128, 16], F32), ("tap_xn1", [128, 2048], BF16),
            ("tap_P1", [128, 8192], BF16), ("tap_P2", [128, 8192], BF16),
            ("tap_comb", [128, 2048], BF16), ("tap_P3", [128, 8192], BF16),
            ("tap_ao", [128, 2048], F32), ("tap_x1", [128, 8192], F32),
            ("tap_lg", [128, 24], F32), ("tap_g", [128, 8], BF16),
            ("tap_h", [128, 2048], BF16), ("tap_wact", [128, 64], F32),
            ("tap_cc", [128, 64], F32), ("tap_gB", [128, 2048], BF16),
            ("tap_nrm", [128, 64], F32),
        ]:
            taps[name] = nc.dram_tensor(name, shape, dt, kind="ExternalOutput").ap()

    x_d = nc.dram_tensor("x", [ROWS, HID], F32, kind="ExternalInput").ap()
    wqT_d = nc.dram_tensor("wqT", [HID, HID], F32, kind="ExternalInput").ap()
    woT_d = nc.dram_tensor("woT", [HID, HID], F32, kind="ExternalInput").ap()
    Mh_d = nc.dram_tensor("Mh", [H, 128, NB * 128], F32, kind="ExternalInput").ap()
    mn_d = nc.dram_tensor("mnT", [H, 128, NB], F32, kind="ExternalInput").ap()
    bw_d = nc.dram_tensor("bw", [1, H * NB], F32, kind="ExternalInput").ap()
    AT_d = nc.dram_tensor("AT", [HID, 512], F32, kind="ExternalInput").ap()
    BT_d = nc.dram_tensor("BT", [512, HID], F32, kind="ExternalInput").ap()
    wrT_d = nc.dram_tensor("wrT", [HID, E8], F32, kind="ExternalInput").ap()
    br_d = nc.dram_tensor("br", [1, E8], F32, kind="ExternalInput").ap()
    id_d = nc.dram_tensor("ident", [128, 128], BF16, kind="ExternalInput").ap()
    e8_d = nc.dram_tensor("e8sel", [E8, 512], BF16, kind="ExternalInput").ap()
    ind_d = nc.dram_tensor("ind64", [64, NB], F32, kind="ExternalInput").ap()
    on512_d = nc.dram_tensor("ones512", [1, 128], F32, kind="ExternalInput").ap()
    on128b_d = nc.dram_tensor("ones128b", [128, 1], BF16, kind="ExternalInput").ap()
    lnc_d = nc.dram_tensor("lnc", [128, 2], F32, kind="ExternalInput").ap()
    out_d = nc.dram_tensor("out", [ROWS, HID], F32, kind="ExternalOutput").ap()

    wqb_d = nc.dram_tensor("wq_bf", [4, HID, 512], BF16, kind="Internal").ap()
    wob_d = nc.dram_tensor("wo_bf", [4, HID, 512], BF16, kind="Internal").ap()
    mb_d = nc.dram_tensor("m_bf", [H, 128, NB * 128], BF16, kind="Internal").ap()
    ab_d = nc.dram_tensor("a_bf", [NCH, 128, 512], BF16, kind="Internal").ap()

    prog = {k: [] for k in ("sync", "gpsimd", "tensor", "vector", "scalar")}

    def EMIT(q, fn):
        prog[q].append(fn)

    with ExitStack() as ctx:
        sb = lambda name, shape, dt: ctx.enter_context(nc.sbuf_tensor(name, shape, dt))
        M_sb = sb("M_sb", [128, 3 * 512], BF16)
        mn_sb = sb("mn_sb", [128, H * NB], BF16)
        A_sb = sb("A_sb", [128, 3 * 512], BF16)
        B_sb = sb("B_sb", [128, 4 * HID], BF16)
        Bcs_sb = sb("Bcs_sb", [128, 4], BF16)
        Wrh_sb = sb("Wrh_sb", [128, NCH * E8], BF16)
        Wrr_sb = sb("Wrr_sb", [128, NCH * E8], BF16)
        wact_sb = sb("wact_sb", [128, H * NB], F32)
        id_sb = sb("id_sb", [128, 128], BF16)
        e8_sb = sb("e8_sb", [E8, 512], BF16)
        ind_sb = sb("ind_sb", [64, NB], F32)
        br_sb = sb("br_sb", [1, E8], F32)
        bw_sb = sb("bw_sb", [1, H * NB], F32)
        row1_sb = sb("row1_sb", [1, 128], F32)
        onesb_sb = sb("onesb_sb", [128, 1], BF16)
        s64_sb = sb("s64_sb", [64, 1], F32)
        act4_sb = sb("act4_sb", [1, NB], F32)
        lnc_sb = sb("lnc_sb", [128, 2], F32)

        x_sb = sb("x_sb", [128, NSUB * HID], F32)
        P1 = sb("P1", [128, NCH * T], BF16)
        P2 = sb("P2", [128, NCH * T], BF16)
        P3 = sb("P3", [128, NCH * T], BF16)
        stg = [sb(f"stg{i}", [128, HID], BF16) for i in range(4)]
        aoqh_sb = sb("aoqh", [128, 2 * 512], BF16)
        aoql_sb = sb("aoql", [128, 2 * 512], BF16)
        aofL = sb("aofL", [128, 4 * HID], F32)

        def aofv(s):
            return aofL[:, s * HID:(s + 1) * HID]
        escr = sb("escr", [128, 1024], F32)
        xn2f_sb = sb("xn2f", [128, HID], F32)
        gB_sb = sb("gB_sb", [128, 4 * 512], BF16)
        h_sc = gB_sb
        wq_buf = sb("wq_buf", [128, 2 * 4 * 512], BF16)

        sum1_sb = sb("sum1_sb", [128, 2 * NSUB], F32)
        sq_sb = sb("sq_sb", [128, 2 * NSUB], F32)
        st4_sb = sb("st4_sb", [128, 2 * NSUB * 4], F32)
        sum4_sb = sb("sum4_sb", [128, 8], F32)
        nrm_sb = sb("nrm_sb", [128, H * NB], F32)
        cc_sb = sb("cc_sb", [128, NSUB * H * NB], F32)
        ctmp_sb = sb("ctmp_sb", [128, 128], F32)
        scr64_sb = sb("scr64_sb", [128, 64], F32)
        wS_sb = sb("wS_sb", [128, 16], BF16)
        l3_sb = sb("l3_sb", [128, 24], F32)
        lg_sb = sb("lg_sb", [128, E8], F32)
        ze_sb = sb("ze_sb", [128, E8], F32)
        g_sb = sb("g_sb", [128, E8], BF16)
        sm_sb = sb("sm_sb", [128, 2 * NSUB * 4], F32)
        xstg_sb = sb("xstg", [128, HID], F32)
        xstg2_sb = sb("xstg2", [128, HID], F32)
        xrs = [sb(f"xrs{i}", [128, NCH * SUB], BF16) for i in range(2)]
        bnscr = aoql_sb[:].bitcast(F32)
        wqX = aofL[:].bitcast(BF16)

        def gT8v(c0, c1):
            return aoqh_sb[0:8, 512 + c0:512 + c1]

        pb = [ctx.enter_context(nc.psum_tensor(f"pb{i}", [128, 512], F32))
              for i in range(8)]

        class _Ch:
            def __init__(self, name):
                self.h = ctx.enter_context(nc.semaphore(name))
                self.n = 0

            def inc(self, step=1):
                self.n += step
                return self.n

        sd = _Ch("sd")
        sdc = _Ch("sdc")
        sg = _Ch("sg")
        st = _Ch("st")
        sv = _Ch("sv")
        ss = _Ch("ss")
        sp = _Ch("sp")

        _hwm = {}

        def W(q, ch, val):
            if not val:
                return
            key = (q, id(ch))
            if _hwm.get(key, -1) >= val:
                return
            _hwm[key] = val
            EMIT(q, lambda e, c=ch.h, v=val: e.wait_ge(c, v))

        class Rec:

            def __init__(self, q, ch, selfsync):
                self.q, self.ch, self.selfsync = q, ch, selfsync

            def w(self, ch, val):
                W(self.q, ch, val)

            def op(self, meth, *a, inc=None, **k):
                do_inc = self.selfsync if inc is None else inc
                p = self.ch.n if self.selfsync else 0
                v = self.ch.inc() if do_inc else self.ch.n
                def f(e, m=meth, a=a, k=k, c=self.ch.h, p=p, di=do_inc):
                    if p:
                        e.wait_ge(c, p)
                    ins = getattr(e, m)(*a, **k)
                    if di:
                        ins.then_inc(c, 1)
                EMIT(self.q, f)
                return v

        V = Rec("vector", sv, True)
        SC = Rec("scalar", ss, True)
        TE = Rec("tensor", st, False)
        G = Rec("gpsimd", sp, True)

        def DMA(q, out_ap, in_ap, **kw):
            ch = {"sync": sd, "scalar": sdc, "gpsimd": sg}[q]
            if _DMA_SERIAL:
                W(q, ch, ch.n)
            v = ch.inc(16)
            def f(e, o=out_ap, i=in_ap, c=ch.h, kw=kw):
                e.dma_start(o, i, **kw).then_inc(c, 16)
            EMIT(q, f)
            return v

        def TAP(name, src_ap, waits):
            if not _TAPS or name not in taps:
                return
            for ch, v in waits:
                W("sync", ch, v)
            tv = DMA("sync", taps[name], src_ap)
            taps.pop(name)
            for q in ("vector", "scalar", "tensor", "gpsimd"):
                W(q, sd, tv)

        def slab(buf, k):
            return buf[:, k * T:(k + 1) * T]

        def subv(buf, s):
            return buf[:, s * HID:(s + 1) * HID]

        def fwdT_out(buf, s):
            return buf[:].rearrange("p (c t) -> p c t", c=NCH)[:, :, s * SUB:(s + 1) * SUB]

        def revT_out(buf, gc):
            return buf[:].rearrange("p (s f) -> p s f", s=NSUB)[:, :, gc * SUB:(gc + 1) * SUB]

        def stat(s, j):
            return st4_sb[:, s * 4 + j:s * 4 + j + 1]

        def smc(s, j):
            return sm_sb[:, s * 4 + j:s * 4 + j + 1]

        wq_cast_tok = [DMA("gpsimd", wqb_d[Q], wqT_d[:, Q * 512:(Q + 1) * 512])
                       for Q in range(4)]

        tok_id = DMA("sync", id_sb[:], id_d)
        DMA("sync", e8_sb[:], e8_d)
        tok_ind = DMA("sync", ind_sb[:], ind_d)
        DMA("sync", br_sb[:], br_d)
        tok_bw = DMA("sync", bw_sb[:], bw_d)
        DMA("sync", row1_sb[:], on512_d)
        tok_ones = DMA("sync", onesb_sb[:], on128b_d)
        tok_lnc = DMA("sync", lnc_sb[:], lnc_d)
        W("scalar", sd, tok_lnc)
        Wrf_sb = xn2f_sb
        tok_wr = DMA("sync", Wrf_sb[:, 0:128].rearrange("p (k e) -> p k e", k=NCH),
                     wrT_d.rearrange("(k p) e -> p k e", p=128))

        cast_mb = DMA("gpsimd", mb_d, Mh_d)
        cast_ab = DMA("gpsimd", ab_d, AT_d.rearrange("(k p) j -> k p j", p=128))
        pre_cast_all = cast_ab

        tok_mn = DMA("sync", x_sb[:, 0:H * NB].rearrange("p (h n) -> p h n", h=H),
                     mn_d.rearrange("h p n -> p h n"))
        V.w(sd, tok_mn)
        cast_mn = V.op("tensor_copy", mn_sb[:], x_sb[:, 0:H * NB])
        W("sync", sv, cast_mn)
        tok_b = DMA("sync", x_sb[:, 0:4 * HID].rearrange("p (q j) -> p q j", q=4),
                    BT_d.rearrange("(q p) j -> p q j", p=128))
        V.w(sd, tok_b)
        cast_b = V.op("tensor_copy", B_sb[:], x_sb[:, 0:4 * HID])
        W("sync", sv, cast_b)
        xload_tok = {}
        for s in range(NSUB):
            xload_tok[(0, s)] = DMA(
                "sync", x_sb[:].rearrange("p (s f) -> p s f", s=NSUB)[:, s, :],
                x_d[s * SUB:(s + 1) * SUB, :])

        V.w(sd, tok_wr)
        V.op("tensor_copy", Wrh_sb[:], Wrf_sb[:, 0:128])
        V.op("tensor_sub", Wrr_sb[:], Wrf_sb[:, 0:128], Wrh_sb[:])
        for p in range(4):
            V.op("reduce_sum", nrm_sb[:, p:p + 1], B_sb[:, p * HID:(p + 1) * HID],
                 axis=AX.X)
        vec_bcs = V.op("tensor_copy", Bcs_sb[:], nrm_sb[:, 0:4])

        for c in range(NCH):
            W("sync", sv, sv.n)
            tok_ws = DMA("sync", xn2f_sb[:],
                         woT_d[c * 128:(c + 1) * 128, :])
            V.w(sd, tok_ws)
            V.op("reduce_sum", cc_sb[:, c:c + 1], xn2f_sb[:], axis=AX.X)
        vec_ws = V.op("tensor_copy", wS_sb[:], cc_sb[:, 0:16])


        TE.w(sd, tok_bw)
        TE.w(sd, tok_ones)
        mm_bwB = TE.op("matmul", pb[5][:, 0:64], row1_sb[0:1, 0:128],
                       bw_sb[0:1, :], start=True, stop=True, inc=True)
        TE.w(sv, cast_mn)
        mm_act1 = TE.op("matmul", pb[7][0:64, 0:1], mn_sb[:, 0:64], onesb_sb[:],
                        start=True, stop=True, inc=True)
        V.w(st, mm_act1)
        vec_s64 = V.op("tensor_copy", s64_sb[:], pb[7][0:64, 0:1])
        TE.w(sv, vec_s64)
        TE.w(sd, tok_ind)
        mm_act2 = TE.op("matmul", pb[6][0:1, 0:NB], s64_sb[:], ind_sb[:],
                        start=True, stop=True, inc=True)
        V.w(st, mm_act2)
        vec_act4 = V.op("tensor_single_scalar", act4_sb[0:1, :],
                        pb[6][0:1, 0:NB], EPS_A, ALU.is_ge)
        TE.w(sv, vec_act4)
        mm_actB = TE.op("matmul", pb[6][:, 0:NB], row1_sb[0:1, 0:128],
                        act4_sb[0:1, :], start=True, stop=True, inc=True)

        V.w(st, mm_bwB)
        vec_bwB = V.op("tensor_copy", wact_sb[:], pb[5][:, 0:64])
        w3i = wact_sb[:].rearrange("p (h n) -> p h n", n=NB)
        w3c = cc_sb[:, 0:64].rearrange("p (h n) -> p h n", n=NB)
        V.op("reduce_max", nrm_sb[:, 0:16].rearrange("p (h o) -> p h o", o=1),
             w3i, axis=AX.X)
        mbc = nrm_sb[:, 0:16].rearrange("p (h o) -> p h o", o=1) \
                             .to_broadcast([128, H, NB])
        vec_sm1 = V.op("tensor_tensor", w3c, w3i, mbc, ALU.subtract)
        SC.w(sv, vec_sm1)
        sc_sm = SC.op("activation", cc_sb[:, 0:64], cc_sb[:, 0:64], AF.Exp)
        V.w(ss, sc_sm)
        V.w(st, mm_actB)
        V.op("reduce_sum", nrm_sb[:, 16:32].rearrange("p (h o) -> p h o", o=1),
             w3c, axis=AX.X)
        V.op("reciprocal", nrm_sb[:, 16:32], nrm_sb[:, 16:32])
        zbc = nrm_sb[:, 16:32].rearrange("p (h o) -> p h o", o=1) \
                              .to_broadcast([128, H, NB])
        V.op("tensor_tensor", w3i, w3c, zbc, ALU.mult)
        V.op("tensor_copy", nrm_sb[:, 32:36], pb[6][:, 0:NB])
        abc = nrm_sb[:, 32:36].unsqueeze(1).to_broadcast([128, H, NB])
        vec_wactb = V.op("tensor_tensor", w3i, w3i, abc, ALU.mult)
        TAP("tap_wact", wact_sb[:], [(sv, vec_wactb)])

        p1_reader_tok = 0
        p2_reader_tok = 0
        p3_reader_tok = 0
        stg_free = [0, 0, 0, 0]
        bank_free = [(st, 0)] * 8
        bank_free[5] = (sv, vec_bwB)
        bank_free[6] = (sv, vec_wactb)
        bank_free[7] = (sv, vec_s64)
        aoq_tr_tok = 0
        prev_stream_tok = 0
        wo_cast_tok = []
        prev_x2_tok = 0
        cc_act_reader = [0]
        ln3_red_tok = {}
        ln3_red_prev = [0]
        ln2_x1_prev = [0]
        xn2f_red_prev = [0]
        xstg_guard = [(0, 0, 0), (0, 0, 0)]

        store_tok = {}
        tr_xn1 = {}

        def wait_bank(rec, b):
            ch, tok = bank_free[b]
            rec.w(ch, tok)

        def wslab(par, c0, c1):
            if par < 2:
                return wq_buf[:, par * 2048 + c0:par * 2048 + c1]
            return wqX[:, (par - 2) * 2048 + c0:(par - 2) * 2048 + c1]

        def group_load(src_d, cast_toks, g, prev_tok, nbuf=2):
            Q, kg = g // 4, g % 4
            par = g % nbuf
            W("scalar", st, prev_tok)
            if cast_toks:
                W("scalar", sg, cast_toks[Q])
            return DMA("scalar",
                       wslab(par, 0, 2048).rearrange("p (k j) -> p k j", k=4),
                       src_d[Q, kg * 512:(kg + 1) * 512, :]
                       .rearrange("(k p) j -> p k j", p=128))

        def stream_phase(src_d, cast_toks, rhs_buf, rhs_ready, evac_emit,
                         pre_ld=None, nbuf=2):
            nonlocal prev_stream_tok
            grp_done = {}
            ld_tok = dict(pre_ld or {})
            for Q in range(4):
                for kg in range(4):
                    g = Q * 4 + kg
                    par = g % nbuf
                    if g not in ld_tok:
                        ld_tok[g] = group_load(src_d, cast_toks, g,
                                               grp_done.get(g - nbuf,
                                                            prev_stream_tok),
                                               nbuf)
                    if g == 0:
                        TE.w(sd, rhs_ready)
                    if kg == 0:
                        for fc in range(4):
                            wait_bank(TE, (Q % 2) * 4 + fc)
                    TE.w(sdc, ld_tok[g])
                    for kl in range(4):
                        k = kg * 4 + kl
                        for fc in range(4):
                            last = (kl == 3 and fc == 3)
                            TE.op("matmul", pb[(Q % 2) * 4 + fc][:],
                                  wslab(par, kl * 512 + fc * 128,
                                        kl * 512 + (fc + 1) * 128),
                                  slab(rhs_buf, k),
                                  start=(k == 0), stop=(k == 15), inc=last)
                    grp_done[g] = st.n
                    if g + nbuf <= 15 and (g + nbuf) not in ld_tok:
                        ld_tok[g + nbuf] = group_load(src_d, cast_toks,
                                                      g + nbuf, grp_done[g],
                                                      nbuf)
                for fc in range(4):
                    b = (Q % 2) * 4 + fc
                    bank_free[b] = evac_emit(Q * 4 + fc, b, grp_done[Q * 4 + 3])
            prev_stream_tok = grp_done[15]
            return grp_done[15]


        def stat1(s, j):
            return st4_sb[:, 16 + s * 4 + j:16 + s * 4 + j + 1]

        def smc1(s, j):
            return sm_sb[:, 16 + s * 4 + j:16 + s * 4 + j + 1]

        def sum1c(mm, s):
            return sum1_sb[:, (mm % 2) * 4 + s:(mm % 2) * 4 + s + 1]

        def xstg_load(mm, s):
            buf = xstg_sb if s % 2 == 0 else xstg2_sb
            W("scalar", ss, xstg_guard[s % 2][0])
            W("scalar", sv, xstg_guard[s % 2][1])
            W("scalar", sp, xstg_guard[s % 2][2])
            return DMA("scalar", buf[:],
                       x_d[mm * T + s * SUB: mm * T + (s + 1) * SUB, :])

        def emit_ln1_sub(mm, s, xst):
            if mm == 0:
                xsrc = subv(x_sb, s)
                xch = sd
            else:
                xsrc = (xstg_sb if s % 2 == 0 else xstg2_sb)[:]
                xch = sdc
            V.w(xch, xst)
            for c in range(4):
                V.op("bn_stats", ctmp_sb[:, c * 6:(c + 1) * 6],
                     xsrc.rearrange("p (c f) -> p c f", c=4)[:, c, :])
            t_var = V.op("bn_aggr", st4_sb[:, 16 + s * 4:16 + s * 4 + 2],
                         ctmp_sb[:, 0:24])
            ln1_red_tok[s] = t_var
            SC.w(xch, xst)
            SC.w(sv, t_var)
            SC.op("activation", stat1(s, 2), stat1(s, 1), AF.Ln,
                  bias=lnc_sb[:, 0:1])
            t_rstd = SC.op("activation", stat1(s, 2), stat1(s, 2), AF.Exp,
                           scale=-0.5)
            V.w(ss, t_rstd)
            t_negmr = V.op("tensor_scalar", stat1(s, 3), stat1(s, 2),
                           stat1(s, 0), -1.0, ALU.mult, ALU.mult)
            SC.w(sd, stg_free[s])
            SC.w(sv, t_negmr)
            t_ap = SC.op("activation", stg[s][:], xsrc, AF.Identity,
                         bias=stat1(s, 3), scale=stat1(s, 2))
            if mm > 0:
                xstg_guard[s % 2] = (t_ap, t_var, 0)
            return t_ap

        def emit_ln1_tr(mm, s, t_ap, p1_guard=0):
            W("sync", ss, t_ap)
            if mm == 0:
                if s == 0:
                    W("sync", sg, pre_cast_all)
            else:
                W("sync", sp, p1_guard)
            tok = DMA("sync", fwdT_out(P1, s), stg[s][:], transpose=True)
            stg_free[s] = tok
            return tok

        def emit_q(mm, xn1_trs, p2_guard, escr_guard=0, pre_ld=None,
                   mid=None):
            if p2_guard:
                W("gpsimd", st, p2_guard)
            elu_guard = {}
            mid = mid or {}

            def elu_evac(gc, b, mm_tok):
                if gc % 2 == 0:
                    scrE = escr[:, 0:512]
                    scrR = escr[:, 512:1024]
                else:
                    scrE = xstg_sb[:, 0:512]
                    scrR = xstg_sb[:, 512:1024]
                SC.w(st, mm_tok)
                if gc == 0:
                    SC.w(sv, escr_guard)
                SC.w(sp, elu_guard.get(gc - 2, 0))
                SC.op("activation", scrR, pb[b][:], AF.Relu, scale=-1.0)
                SC.op("activation", scrE, scrR, AF.Exp, scale=-1.0)
                t_e1 = SC.op("activation", scrR, pb[b][:], AF.Relu)
                G.w(ss, t_e1)
                t_e2 = G.op("tensor_tensor", slab(P2, gc), scrE, scrR,
                            ALU.add)
                elu_guard[gc] = t_e2
                if gc in mid:
                    mid[gc]()
                return (ss, t_e1)

            q_last = stream_phase(wqb_d, wq_cast_tok if mm == 0 else None,
                                  P1, xn1_trs[NSUB - 1], elu_evac,
                                  pre_ld=pre_ld, nbuf=4)
            if _TAPS and mm == 1:
                TAP("tap_P2", P2[:], [(sp, sp.n)])
            return q_last, sp.n

        p1_free_tok = [0]
        ln1_red_tok = {0: 0, 1: 0, 2: 0, 3: 0}
        m_rd_prev = {0: 0, 1: 0, 2: 0}
        a_rd_prev = {0: 0, 1: 0, 2: 0}
        tr0 = {}
        for s in range(NSUB):
            _ap0 = emit_ln1_sub(0, s, xload_tok[(0, s)])
            tr0[s] = emit_ln1_tr(0, s, _ap0)
        if True:
            W("gpsimd", sd, tr0[NSUB - 1])
            for Q in range(4):
                wo_cast_tok.append(
                    DMA("gpsimd", wob_d[Q], woT_d[:, Q * 512:(Q + 1) * 512]))
        q_last_mm, sig_done = emit_q(0, tr0, 0)

        for m in range(NMEG):
            p1_q_reader = q_last_mm

            comb_tr = {}
            wo_pre = {}
            t_cc4 = {}
            for s in range(NSUB):
                wait_bank(TE, 6)
                TE.w(sp, sig_done)
                for h in range(H):
                    TE.op("matmul", pb[6][:, h * 4:(h + 1) * 4],
                          P2[:, h * T + s * SUB: h * T + (s + 1) * SUB],
                          mn_sb[:, h * 4:(h + 1) * 4],
                          start=(h == 0), stop=(h == H - 1),
                          skip_group_check=True, inc=(h == H - 1))
                t_norm = st.n
                V.w(st, t_norm)
                V.w(sv, vec_wactb)
                V.w(ss, cc_act_reader[0])
                V.op("tensor_scalar_max", nrm_sb[:], pb[6][:, 0:64], EPS_A)
                V.op("reciprocal", nrm_sb[:], nrm_sb[:])
                t_cc4[s] = V.op("tensor_mul", cc_sb[:, s * 64:(s + 1) * 64],
                                nrm_sb[:], wact_sb[:])
                bank_free[6] = (sv, t_cc4[s])
            for s in range(NSUB):
                W("vector", sd, stg_free[s])
                W("scalar", sd, stg_free[s])
                W("gpsimd", sd, stg_free[s])
            m_rd_tok = {}
            comb_sv = {}
            comb_sp = {}
            for h in range(H):
                par = h % 3
                if m == 0:
                    W("sync", sg, cast_mb)
                W("sync", st, m_rd_tok.get(h - 3, m_rd_prev[par]))
                mld = DMA("sync", M_sb[:, par * 512:(par + 1) * 512], mb_d[h])
                TE.w(sd, mld)
                for s in range(NSUB):
                    bb = (h * 4 + s) % 3
                    wait_bank(TE, bb)
                    t_a = TE.op("matmul", pb[bb][:],
                                P2[:, h * T + s * SUB: h * T + (s + 1) * SUB],
                                M_sb[:, par * 512:(par + 1) * 512],
                                start=True, stop=True, inc=True)
                    if s == NSUB - 1:
                        m_rd_tok[h] = t_a
                        m_rd_prev[par] = t_a
                    ccol = lambda n, h=h, s=s: \
                        cc_sb[:, s * 64 + h * 4 + n:s * 64 + h * 4 + n + 1]
                    if s % 2 == 0:
                        V.w(st, t_a)
                        V.w(sv, t_cc4[s])
                        V.op("tensor_scalar", ctmp_sb[:], pb[bb][:, 0:128],
                             ccol(0), 0.0, ALU.mult, ALU.add)
                        V.op("scalar_tensor_tensor", ctmp_sb[:],
                             pb[bb][:, 128:256], ccol(1), ctmp_sb[:], ALU.mult,
                             ALU.add)
                        V.op("scalar_tensor_tensor", ctmp_sb[:],
                             pb[bb][:, 256:384], ccol(2), ctmp_sb[:], ALU.mult,
                             ALU.add)
                        t_comb = V.op("scalar_tensor_tensor",
                                      stg[s][:, h * 128:(h + 1) * 128],
                                      pb[bb][:, 384:512], ccol(3), ctmp_sb[:],
                                      ALU.mult, ALU.add)
                        bank_free[bb] = (sv, t_comb)
                        comb_sv[s] = t_comb
                    else:
                        so = (s // 2) * 512
                        SC.w(st, t_a)
                        SC.w(sv, t_cc4[s])
                        SC.w(sv, comb_sv.get(s, 0))
                        for n in range(4):
                            t_sc = SC.op("activation",
                                         escr[:, so + n * 128:so + (n + 1) * 128],
                                         pb[bb][:, n * 128:(n + 1) * 128],
                                         AF.Identity, scale=ccol(n))
                        bank_free[bb] = (ss, t_sc)
                        cc_act_reader[0] = t_sc
                        V.w(ss, t_sc)
                        e3 = escr[:, so:so + 512].rearrange(
                            "p (n f) -> p n f", n=4)
                        V.op("tensor_tensor", ctmp_sb[:],
                             e3[:, 0, :], e3[:, 2, :], ALU.add)
                        V.op("scalar_tensor_tensor", ctmp_sb[:],
                             e3[:, 1, :], 1.0, ctmp_sb[:], ALU.mult, ALU.add)
                        t_comb = V.op("scalar_tensor_tensor",
                                      stg[s][:, h * 128:(h + 1) * 128],
                                      e3[:, 3, :], 1.0, ctmp_sb[:],
                                      ALU.mult, ALU.add)
                        comb_sv[s] = t_comb
            for s in range(NSUB):
                if s == 0:
                    W("sync", st, p3_reader_tok)
                    if m == 0:
                        W("sync", sg, wo_cast_tok[-1])
                W("sync", sv, comb_sv[s])
                tok = DMA("sync", fwdT_out(P3, s), stg[s][:], transpose=True)
                comb_tr[s] = tok
                stg_free[s] = tok
                TE.w(sd, tok)
                if m == 0 and s == 0:
                    TE.w(sv, vec_ws)
                for k in range(NCH):
                    t_smm = TE.op("matmul", pb[6][:, 64 + s:65 + s],
                                  P3[:, k * T + s * SUB:k * T + (s + 1) * SUB],
                                  wS_sb[:, k:k + 1],
                                  start=(k == 0), stop=(k == NCH - 1),
                                  skip_group_check=True, inc=(k == NCH - 1))
                if s == NSUB - 1:
                    SC.w(st, t_smm)
                    SC.w(sv, ln2_x1_prev[0])
                    t_scp = SC.op("activation", sum4_sb[:, 0:4],
                                  pb[6][:, 64:68], AF.Identity)
                    bank_free[6] = (ss, t_scp)
                if s < 2:
                    wo_pre[s] = group_load(
                        wob_d, wo_cast_tok if m == 0 else None, s,
                        prev_stream_tok)
                if _TAPS and m == 1 and s == 0:
                    TAP("tap_comb", stg[0][:], [])
                    stg_free[0] = sd.n
                if _TAPS and m == 1 and s == 3:
                    TAP("tap_P3", P3[:], [])
            bank_last_mm = st.n

            xstg_tok = {}
            ln1_ap = {}
            wo_mid = {}
            if m + 1 < NMEG:
                for s2 in range(2):
                    g0, g1, _ = xstg_guard[s2 % 2]
                    xstg_guard[s2 % 2] = (g0, g1, sig_done)
                    xstg_tok[s2] = xstg_load(m + 1, s2)

                def _ln1_mid(s):
                    def f():
                        ln1_ap[s] = emit_ln1_sub(m + 1, s, xstg_tok[s])
                        if s + 2 < NSUB:
                            xstg_tok[s + 2] = xstg_load(m + 1, s + 2)
                    return f
                wo_mid = {}

            revtr_last = [aoq_tr_tok]
            incr_g = {}
            incr_v = {}
            tr_pair = {}
            aofL3 = aofL[:].rearrange("p (s f) -> p s f", s=NSUB)
            P1r3 = P1[:].rearrange("p (s f) -> p s f", s=NSUB)
            P2r3 = P2[:].rearrange("p (s f) -> p s f", s=NSUB)

            def wo_evac(gc, b, mm_tok):
                fc2 = gc % 2
                SC.w(st, mm_tok)
                SC.w(sd, revtr_last[0])
                t_hi = SC.op("activation", aoqh_sb[:, fc2 * 512:(fc2 + 1) * 512],
                             pb[b][:], AF.Identity)
                V.w(ss, t_hi)
                V.w(sd, revtr_last[0])
                t_lo = V.op("tensor_sub", aoql_sb[:, fc2 * 512:(fc2 + 1) * 512],
                            pb[b][:], aoqh_sb[:, fc2 * 512:(fc2 + 1) * 512])
                if fc2 == 1:
                    W("sync", sv, t_lo)
                    W("sync", ss, t_hi)
                    if gc == 1:
                        W("sync", st, p1_q_reader)
                        W("sync", st, bank_last_mm)
                        if m == 0:
                            W("sync", sg, wo_cast_tok[-1])
                    tok = None
                    for c in (0, 1):
                        gcc = gc - 1 + c
                        DMA("sync", revT_out(P1, gcc),
                            aoqh_sb[:, c * 512:(c + 1) * 512], transpose=True)
                        tok = DMA("sync", revT_out(P2, gcc),
                                  aoql_sb[:, c * 512:(c + 1) * 512],
                                  transpose=True)
                    revtr_last[0] = tok
                    pi = gc // 2
                    tr_pair[pi] = tok
                    if pi >= 1:
                        emit_incr(pi - 1)
                if gc in wo_mid:
                    wo_mid[gc]()
                return (sv, t_lo)

            def emit_incr(pi):
                for gcc in (2 * pi, 2 * pi + 1):
                    G.w(sd, tr_pair[pi])
                    if gcc == 0:
                        G.w(sv, ln3_red_prev[0])
                    t_g = G.op("tensor_tensor",
                               aofL3[:, :, gcc * SUB:(gcc + 1) * SUB],
                               P1r3[:, :, gcc * SUB:(gcc + 1) * SUB],
                               P2r3[:, :, gcc * SUB:(gcc + 1) * SUB],
                               ALU.add)
                    incr_g[gcc] = t_g
                    par = gcc % 2
                    SC.w(sp, t_g)
                    if gcc == 0:
                        SC.w(sp, sig_done)
                        SC.w(sv, comb_sv[1])
                        SC.w(sv, comb_sv[3])
                    if gcc - 2 in incr_v:
                        SC.w(sv, incr_v[gcc - 2])
                    sq3 = escr[:, par * 512:(par + 1) * 512].rearrange(
                        "p (s f) -> p s f", s=NSUB)
                    t_sq = SC.op("activation", sq3,
                                 aofL3[:, :, gcc * SUB:(gcc + 1) * SUB],
                                 AF.Square)
                    V.w(ss, t_sq)
                    t_rd = V.op("reduce_sum",
                                scr64_sb[:, gcc * 4:(gcc + 1) * 4]
                                .rearrange("p (s o) -> p s o", o=1),
                                sq3, axis=AX.X)
                    incr_v[gcc] = t_rd

            wo_last_mm = stream_phase(wob_d, wo_cast_tok if m == 0 else None,
                                      P3, comb_tr[NSUB - 1], wo_evac,
                                      pre_ld=wo_pre)
            aoq_tr_tok = revtr_last[0]
            q_pre = {}
            if m + 1 < NMEG:
                for gq in range(2):
                    q_pre[gq] = group_load(wqb_d, None, gq, prev_stream_tok)
            emit_incr(7)
            recomb_last = incr_g[NCH - 1]
            incr_red_last = incr_v[NCH - 1]
            if _TAPS and m == 1:
                TAP("tap_ao", aofv(0), [(sp, recomb_last)])

            trn = {}
            if m + 1 < NMEG:
                for s in range(NSUB):
                    ln1_ap[s] = emit_ln1_sub(m + 1, s, xstg_tok[s])
                    if s + 2 < NSUB:
                        xstg_tok[s + 2] = xstg_load(m + 1, s + 2)
                    trn[s] = emit_ln1_tr(m + 1, s, ln1_ap[s], recomb_last)

            x1_tok = {}
            V.w(sp, recomb_last)
            V.w(ss, t_scp)
            V.op("tensor_add", scr64_sb[:, 0:32], scr64_sb[:, 0:32],
                 scr64_sb[:, 32:64])
            V.op("tensor_add", scr64_sb[:, 0:16], scr64_sb[:, 0:16],
                 scr64_sb[:, 16:32])
            V.op("tensor_add", scr64_sb[:, 0:8], scr64_sb[:, 0:8],
                 scr64_sb[:, 8:16])
            V.op("tensor_add", scr64_sb[:, 0:4], scr64_sb[:, 0:4],
                 scr64_sb[:, 4:8])
            for s in range(NSUB):
                V.op("tensor_single_scalar", stat(s, 0), sum4_sb[:, s:s + 1],
                     1.0 / HID, ALU.mult)
                V.op("tensor_scalar", stat(s, 1), scr64_sb[:, s:s + 1],
                     1.0 / HID, EPS_LN, ALU.mult, ALU.add)
                V.op("tensor_scalar", smc(s, 0), stat(s, 0), stat(s, 0), None,
                     ALU.mult)
                t_var2 = V.op("tensor_sub", stat(s, 1), stat(s, 1), smc(s, 0))
                SC.w(sv, t_var2)
                SC.op("activation", stat(s, 2), stat(s, 1), AF.Ln)
                t_rstd2 = SC.op("activation", stat(s, 2), stat(s, 2), AF.Exp,
                                scale=-0.5)
                V.w(ss, t_rstd2)
                V.op("tensor_scalar", stat(s, 3), stat(s, 2), stat(s, 0),
                     -1.0, ALU.mult, ALU.mult)
                V.w(sd if m == 0 else sdc, xload_tok[(m, s)])
                V.op("tensor_scalar", subv(x_sb, s), subv(x_sb, s),
                     stat(s, 3), None, ALU.add)
                t_stt2 = V.op("scalar_tensor_tensor", subv(x_sb, s),
                              aofv(s), stat(s, 2), subv(x_sb, s),
                              ALU.mult, ALU.add)
                x1_tok[s] = t_stt2
            ln2_x1_prev[0] = x1_tok[NSUB - 1]
            if _TAPS and m == 1:
                TAP("tap_x1", x_sb[:], [(sv, x1_tok[NSUB - 1])])

            xn2_tr = {}
            xres_tr = {}
            t_sqa3 = {}
            t_var3 = {}
            t_rstd3 = {}
            t_negmr3 = {}
            for s in range(NSUB):
                for c in range(4):
                    V.op("bn_stats",
                         bnscr[:, 96 + s * 24 + c * 6:96 + s * 24 + (c + 1) * 6],
                         subv(x_sb, s).rearrange("p (c f) -> p c f", c=4)[:, c, :])
                t_var3[s] = V.op("bn_aggr", sm_sb[:, 16 + s * 4:16 + s * 4 + 2],
                                 bnscr[:, 96 + s * 24:96 + (s + 1) * 24])
                ln3_red_tok[s] = t_var3[s]
            def ln3_fin(s):
                SC.w(sv, t_var3[s])
                SC.op("activation", stat(s, 2), sm_sb[:, 16 + s * 4 + 1:
                                                      16 + s * 4 + 2],
                      AF.Ln, bias=lnc_sb[:, 0:1])
                t_rstd3[s] = SC.op("activation", stat(s, 2), stat(s, 2),
                                   AF.Exp, scale=-0.5)
                V.w(ss, t_rstd3[s])
                t_negmr3[s] = V.op("tensor_scalar", stat(s, 3), stat(s, 2),
                                   sm_sb[:, 16 + s * 4:16 + s * 4 + 1],
                                   -1.0, ALU.mult, ALU.mult)
            t_apf3 = {}
            t_res3 = {}

            def ln3_apply(s):
                SC.w(sd, stg_free[s % 2])
                SC.w(sv, t_negmr3[s])
                SC.op("activation", stg[s % 2][:], subv(x_sb, s), AF.Identity,
                      bias=stat(s, 3), scale=stat(s, 2))
                if s >= 1:
                    SC.w(sv, t_res3.get(s - 1, 0))
                t_apf3[s] = SC.op("activation", xn2f_sb[:], subv(x_sb, s),
                                  AF.Identity, bias=stat(s, 3),
                                  scale=stat(s, 2))
                V.w(sd, stg_free[2 + s % 2])
                V.w(ss, t_apf3[s])
                t_res3[s] = V.op("tensor_sub", stg[2 + s % 2][:], xn2f_sb[:],
                                 stg[s % 2][:])
                W("sync", ss, t_apf3[s])
                if s == 0:
                    W("sync", st, wo_last_mm)
                tok1 = DMA("sync", fwdT_out(P3, s), stg[s % 2][:],
                           transpose=True)
                xn2_tr[s] = tok1
                stg_free[s % 2] = tok1
                if s < 2:
                    W("sync", sv, t_res3[s])
                    tok2 = DMA("sync",
                               xrs[s % 2][:].rearrange("p (c t) -> p c t", c=NCH),
                               stg[2 + s % 2][:], transpose=True)
                    xres_tr[s] = tok2
                    stg_free[2 + s % 2] = tok2
            ln3_red_prev[0] = x1_tok[NSUB - 1]

            if m + 1 < NMEG:
                W("scalar", sv, x1_tok[0])
                q_pre[2] = group_load(wqb_d, None, 2, prev_stream_tok, 4)
                q_pre[3] = group_load(wqb_d, None, 3, prev_stream_tok, 4)
                q_last_next, sig_next = emit_q(
                    m + 1, trn, bank_last_mm, incr_red_last, pre_ld=q_pre,
                    mid={3: lambda: ln3_fin(0), 4: lambda: ln3_fin(1),
                         5: lambda: ln3_fin(2), 6: lambda: ln3_fin(3),
                         7: lambda: ln3_apply(0), 9: lambda: ln3_apply(1),
                         11: lambda: ln3_apply(2), 13: lambda: ln3_apply(3)})
            else:
                for s in range(NSUB):
                    ln3_fin(s)
                    ln3_apply(s)
                q_last_next, sig_next = 0, 0

            g_tok = {}
            rt_tok = {}
            for s in range(NSUB):
                if s >= 2:
                    W("sync", sv, t_res3[s])
                    W("sync", st, rt_tok[s - 2])
                    tok2 = DMA("sync",
                               xrs[s % 2][:].rearrange("p (c t) -> p c t", c=NCH),
                               stg[2 + s % 2][:], transpose=True)
                    xres_tr[s] = tok2
                    stg_free[2 + s % 2] = tok2
                wait_bank(TE, 3)
                TE.w(sd, xn2_tr[s])
                TE.w(sd, xres_tr[s])
                first = True
                for k in range(NCH):
                    lhs = P3[:, k * T + s * SUB: k * T + (s + 1) * SUB]
                    TE.op("matmul", pb[3][:, 0:8], lhs,
                          Wrh_sb[:, k * 8:(k + 1) * 8],
                          start=first, stop=False, skip_group_check=True)
                    first = False
                    TE.op("matmul", pb[3][:, 8:16], lhs,
                          Wrr_sb[:, k * 8:(k + 1) * 8],
                          start=False, stop=False, skip_group_check=True)
                    lhr = xrs[s % 2][:, k * 128:(k + 1) * 128]
                    TE.op("matmul", pb[3][:, 16:24], lhr,
                          Wrh_sb[:, k * 8:(k + 1) * 8],
                          start=False, stop=False, skip_group_check=True)
                t_rt = TE.op("matmul", pb[3][:, 0:8], row1_sb[0:1, 0:128],
                             br_sb[0:1, :], start=False, stop=True,
                             skip_group_check=True, inc=True)
                rt_tok[s] = t_rt
                V.w(st, t_rt)
                V.op("tensor_copy", l3_sb[:], pb[3][:, 0:24])
                V.op("tensor_add", lg_sb[:], l3_sb[:, 0:8], l3_sb[:, 8:16])
                V.op("tensor_add", lg_sb[:], lg_sb[:], l3_sb[:, 16:24])
                V.op("reduce_max", smc(s, 0), lg_sb[:], axis=AX.X)
                t_rte = V.op("tensor_single_scalar", smc(s, 1), smc(s, 0),
                             -1.0, ALU.mult)
                bank_free[3] = (sv, t_rte)
                if _TAPS and m == 0 and s == 0:
                    TAP("tap_lg", l3_sb[:], [(sv, t_rte)])
                SC.w(sv, t_rte)
                t_rex = SC.op("activation", ze_sb[:], lg_sb[:], AF.Exp,
                              bias=smc(s, 1))
                V.w(ss, t_rex)
                V.op("reduce_sum", smc(s, 2), ze_sb[:], axis=AX.X)
                V.op("reciprocal", smc(s, 3), smc(s, 2))
                t_g = V.op("tensor_scalar", g_sb[:], lg_sb[:], smc(s, 0),
                           smc(s, 3), ALU.is_ge, ALU.mult)
                wait_bank(TE, 5)
                TE.w(sv, t_g)
                TE.w(sd, tok_id)
                t_gt = TE.op("transpose", pb[5][:].bitcast(BF16)[0:8, 0:128],
                             g_sb[:], id_sb[:], inc=True)
                V.w(st, t_gt)
                g_tok[s] = V.op("tensor_copy", gT8v(s * 128, (s + 1) * 128),
                                pb[5][:].bitcast(BF16)[0:8, 0:128])
                bank_free[5] = (sv, g_tok[s])
                if _TAPS and m == 0 and s == 0:
                    TAP("tap_g", g_sb[:], [(sv, g_tok[s])])

            for p in range(4):
                wait_bank(TE, 4)
                TE.w(sv, g_tok[NSUB - 1])
                t_gb = TE.op("matmul", pb[4][:], e8_sb[:, p * 128:(p + 1) * 128],
                             gT8v(0, 512), start=True, stop=True, inc=True)
                V.w(st, t_gb)
                t_gbe = V.op("tensor_copy", gB_sb[:, p * 512:(p + 1) * 512],
                             pb[4][:])
                bank_free[4] = (sv, t_gbe)
            if _TAPS:
                TAP("tap_gB", gB_sb[:], [(sv, sv.n)])

            h_tok = {}
            a_rd_tok = {}
            for k in range(NCH):
                par = k % 3
                if m == 0:
                    W("sync", sg, cast_ab)
                W("sync", st, a_rd_tok.get(k - 3, a_rd_prev[par]))
                ald = DMA("sync", A_sb[:, par * 512:(par + 1) * 512], ab_d[k])
                TE.w(sd, ald)
                if k == 0:
                    TE.w(sd, xn2_tr[NSUB - 1])
                    for p in range(4):
                        wait_bank(TE, p)
                for p in range(4):
                    TE.op("matmul", pb[p][:],
                          A_sb[:, par * 512 + p * 128:par * 512 + (p + 1) * 128],
                          slab(P3, k), start=(k == 0), stop=(k == NCH - 1),
                          inc=(p == 3))
                a_rd_tok[k] = st.n
                a_rd_prev[par] = st.n
            t_am = st.n
            for p in range(4):
                V.w(st, t_am)
                h_tok[p] = V.op("tensor_mul", h_sc[:, p * 512:(p + 1) * 512],
                                pb[p][:], gB_sb[:, p * 512:(p + 1) * 512])
                bank_free[p] = (sv, h_tok[p])
            if _TAPS:
                TAP("tap_h", h_sc[:], [(sv, h_tok[3])])

            for s in range(NSUB):
                for ob in range(4):
                    b = (s % 2) * 4 + ob
                    wait_bank(TE, b)
                    TE.w(sv, h_tok[3])
                    for p in range(4):
                        TE.op("matmul", pb[b][:],
                              h_sc[:, p * 512 + s * SUB: p * 512 + (s + 1) * SUB],
                              B_sb[:, p * HID + ob * 512: p * HID + (ob + 1) * 512],
                              start=(p == 0), stop=(p == 3), inc=(p == 3))
                t_ys = st.n
                V.w(st, t_ys)
                for ob in range(4):
                    V.op("bn_stats",
                         bnscr[:, 192 + s * 24 + ob * 6:192 + s * 24 + (ob + 1) * 6],
                         pb[(s % 2) * 4 + ob][:])
                t_var4 = V.op("bn_aggr", sm_sb[:, s * 4 + 2:s * 4 + 4],
                              bnscr[:, 192 + s * 24:192 + (s + 1) * 24])
                SC.w(sv, t_var4)
                SC.op("activation", stat(s, 2), smc(s, 3), AF.Ln,
                      bias=lnc_sb[:, 0:1], scale=lnc_sb[:, 1:2])
                t_rstd4 = SC.op("activation", stat(s, 2), stat(s, 2), AF.Exp,
                                scale=-0.5)
                V.w(ss, t_rstd4)
                V.op("tensor_scalar", stat(s, 3), stat(s, 2), smc(s, 2),
                     -SCALING, ALU.mult, ALU.mult)
                t_negmr4 = V.op("tensor_single_scalar", stat(s, 2), stat(s, 2),
                                SCALING, ALU.mult)
                V.op("tensor_scalar", subv(x_sb, s), subv(x_sb, s),
                     stat(s, 3), None, ALU.add)
                t_x2 = 0
                for ob in range(4):
                    t_x2 = V.op("scalar_tensor_tensor",
                                x_sb[:, s * HID + ob * 512:
                                     s * HID + (ob + 1) * 512],
                                pb[(s % 2) * 4 + ob][:], stat(s, 2),
                                x_sb[:, s * HID + ob * 512:
                                     s * HID + (ob + 1) * 512],
                                ALU.mult, ALU.add)
                for ob in range(4):
                    bank_free[(s % 2) * 4 + ob] = (sv, t_x2)
                prev_x2_tok = t_x2

                W("scalar", sv, t_x2)
                store_tok[(m, s)] = DMA(
                    "scalar", out_d[m * T + s * SUB: m * T + (s + 1) * SUB, :],
                    subv(x_sb, s))
                if m + 1 < NMEG:
                    xload_tok[(m + 1, s)] = DMA(
                        "scalar",
                        x_sb[:].rearrange("p (s f) -> p s f", s=NSUB)[:, s, :],
                        x_d[(m + 1) * T + s * SUB:(m + 1) * T + (s + 1) * SUB, :])

            q_last_mm = q_last_next
            sig_done = sig_next
            p2_reader_tok = bank_last_mm
            p3_reader_tok = st.n

        final_sd = sd.n
        EMIT("sync", lambda e, v=final_sd: e.wait_ge(sd.h, v))
        final_sdc = sdc.n
        EMIT("sync", lambda e, v=final_sdc: e.wait_ge(sdc.h, v))

        blk = ctx.enter_context(nc.Block())

        @blk.sync
        def _(e):
            for f in prog["sync"]:
                f(e)

        @blk.gpsimd
        def _(e):
            for f in prog["gpsimd"]:
                f(e)

        @blk.tensor
        def _(e):
            for f in prog["tensor"]:
                f(e)

        @blk.vector
        def _(e):
            for f in prog["vector"]:
                f(e)

        @blk.scalar
        def _(e):
            for f in prog["scalar"]:
                f(e)

    return nc


_NC = None


def kernel(**inputs):
    global _NC
    import ml_dtypes
    x = np.ascontiguousarray(
        np.asarray(inputs["L0_attn_hidden_states"], dtype=np.float32)
        .reshape(NCORE * ROWS, HID))
    wqT = np.ascontiguousarray(np.asarray(inputs["L0_attn_Wq"], np.float32).T)
    woT = np.ascontiguousarray(np.asarray(inputs["L0_attn_Wo"], np.float32).T)
    Mh = np.ascontiguousarray(
        np.asarray(inputs["L0_attn_memories"], np.float32).transpose(1, 2, 0, 3)
        .reshape(H, 128, NB * 128))
    mnT = np.ascontiguousarray(
        np.asarray(inputs["L0_attn_memory_norms"], np.float32).transpose(1, 2, 0))
    bw = np.ascontiguousarray(
        np.asarray(inputs["L0_attn_bank_weights"], np.float32).reshape(1, H * NB))
    AT = np.ascontiguousarray(
        np.asarray(inputs["L0_ffn_A"], np.float32).transpose(2, 0, 1)
        .reshape(HID, 512))
    BT = np.ascontiguousarray(
        np.asarray(inputs["L0_ffn_B"], np.float32).transpose(0, 2, 1)
        .reshape(512, HID))
    wrT = np.ascontiguousarray(np.asarray(inputs["L0_ffn_Wr"], np.float32).T)
    br = np.ascontiguousarray(np.asarray(inputs["L0_ffn_br"], np.float32)
                              .reshape(1, E8))
    import ml_dtypes as _mld
    ident = np.eye(128, dtype=_mld.bfloat16)
    e8sel = np.zeros((E8, 512), dtype=ml_dtypes.bfloat16)
    for p in range(4):
        e8sel[2 * p, p * 128: p * 128 + 64] = 1.0
        e8sel[2 * p + 1, p * 128 + 64: (p + 1) * 128] = 1.0
    ind64 = np.zeros((64, NB), dtype=np.float32)
    for h in range(H):
        for n in range(NB):
            ind64[h * NB + n, n] = 1.0
    ones512 = np.ones((1, 128), dtype=np.float32)
    ones128b = np.ones((128, 1), dtype=ml_dtypes.bfloat16)
    lnc = np.empty((128, 2), dtype=np.float32)
    lnc[:, 0] = EPS_LN
    lnc[:, 1] = SCALING * SCALING

    if _NC is None:
        _NC = build_nc()

    common = dict(wqT=wqT, woT=woT, Mh=Mh, mnT=mnT, bw=bw, AT=AT, BT=BT,
                  wrT=wrT, br=br, ident=ident, e8sel=e8sel, ind64=ind64,
                  ones512=ones512, ones128b=ones128b, lnc=lnc)
    in_maps = [dict(common, x=np.ascontiguousarray(x[c * ROWS:(c + 1) * ROWS]))
               for c in range(NCORE)]
    res = run_bass_kernel_spmd(_NC, in_maps, core_ids=list(range(NCORE)))
    out = np.concatenate([r["out"] for r in res.results], axis=0)
    return out.reshape(4, 4096, HID)


# version 23
# speedup vs baseline: 1.0188x; 1.0188x over previous
import numpy as np
from contextlib import ExitStack

import concourse.bass as bass
import concourse.mybir as mybir
from concourse.bass_utils import run_bass_kernel_spmd

F32 = mybir.dt.float32
BF16 = mybir.dt.bfloat16
AF = mybir.ActivationFunctionType
ALU = mybir.AluOpType
AX = mybir.AxisListType

NCORE = 8
ROWS = 2048
T = 512
NMEG = ROWS // T
NSUB = 4
SUB = 128
HID = 2048
NCH = 16
H = 16
NB = 4
E8 = 8
EPS_LN = 1e-5
EPS_A = 1e-6
SCALING = 1.0 / 64.0

_TAPS = False
_DMA_SERIAL = False


def build_nc():
    nc = bass.Bass("TRN2", target_bir_lowering=False)
    taps = {}
    if _TAPS:
        for name, shape, dt in [
            ("tap_stats", [128, 16], F32), ("tap_xn1", [128, 2048], BF16),
            ("tap_P1", [128, 8192], BF16), ("tap_P2", [128, 8192], BF16),
            ("tap_comb", [128, 2048], BF16), ("tap_P3", [128, 8192], BF16),
            ("tap_ao", [128, 2048], F32), ("tap_x1", [128, 8192], F32),
            ("tap_lg", [128, 24], F32), ("tap_g", [128, 8], BF16),
            ("tap_h", [128, 2048], BF16), ("tap_wact", [128, 64], F32),
            ("tap_cc", [128, 64], F32), ("tap_gB", [128, 2048], BF16),
            ("tap_nrm", [128, 64], F32),
        ]:
            taps[name] = nc.dram_tensor(name, shape, dt, kind="ExternalOutput").ap()

    x_d = nc.dram_tensor("x", [ROWS, HID], F32, kind="ExternalInput").ap()
    wqT_d = nc.dram_tensor("wqT", [HID, HID], F32, kind="ExternalInput").ap()
    woT_d = nc.dram_tensor("woT", [HID, HID], F32, kind="ExternalInput").ap()
    Mh_d = nc.dram_tensor("Mh", [H, 128, NB * 128], F32, kind="ExternalInput").ap()
    mn_d = nc.dram_tensor("mnT", [H, 128, NB], F32, kind="ExternalInput").ap()
    bw_d = nc.dram_tensor("bw", [1, H * NB], F32, kind="ExternalInput").ap()
    AT_d = nc.dram_tensor("AT", [HID, 512], F32, kind="ExternalInput").ap()
    BT_d = nc.dram_tensor("BT", [512, HID], F32, kind="ExternalInput").ap()
    wrT_d = nc.dram_tensor("wrT", [HID, E8], F32, kind="ExternalInput").ap()
    br_d = nc.dram_tensor("br", [1, E8], F32, kind="ExternalInput").ap()
    id_d = nc.dram_tensor("ident", [128, 128], BF16, kind="ExternalInput").ap()
    e8_d = nc.dram_tensor("e8sel", [E8, 512], BF16, kind="ExternalInput").ap()
    ind_d = nc.dram_tensor("ind64", [64, NB], F32, kind="ExternalInput").ap()
    on512_d = nc.dram_tensor("ones512", [1, 128], F32, kind="ExternalInput").ap()
    on128b_d = nc.dram_tensor("ones128b", [128, 1], BF16, kind="ExternalInput").ap()
    lnc_d = nc.dram_tensor("lnc", [128, 2], F32, kind="ExternalInput").ap()
    out_d = nc.dram_tensor("out", [ROWS, HID], F32, kind="ExternalOutput").ap()

    wqb_d = nc.dram_tensor("wq_bf", [4, HID, 512], BF16, kind="Internal").ap()
    wob_d = nc.dram_tensor("wo_bf", [4, HID, 512], BF16, kind="Internal").ap()
    mb_d = nc.dram_tensor("m_bf", [H, 128, NB * 128], BF16, kind="Internal").ap()
    ab_d = nc.dram_tensor("a_bf", [NCH, 128, 512], BF16, kind="Internal").ap()

    prog = {k: [] for k in ("sync", "gpsimd", "tensor", "vector", "scalar")}

    def EMIT(q, fn):
        prog[q].append(fn)

    with ExitStack() as ctx:
        sb = lambda name, shape, dt: ctx.enter_context(nc.sbuf_tensor(name, shape, dt))
        M_sb = sb("M_sb", [128, 3 * 512], BF16)
        mn_sb = sb("mn_sb", [128, H * NB], BF16)
        A_sb = sb("A_sb", [128, 3 * 512], BF16)
        B_sb = sb("B_sb", [128, 4 * HID], BF16)
        Bcs_sb = sb("Bcs_sb", [128, 4], BF16)
        Wrh_sb = sb("Wrh_sb", [128, NCH * E8], BF16)
        Wrr_sb = sb("Wrr_sb", [128, NCH * E8], BF16)
        wact_sb = sb("wact_sb", [128, H * NB], F32)
        id_sb = sb("id_sb", [128, 128], BF16)
        e8_sb = sb("e8_sb", [E8, 512], BF16)
        ind_sb = sb("ind_sb", [64, NB], F32)
        br_sb = sb("br_sb", [1, E8], F32)
        bw_sb = sb("bw_sb", [1, H * NB], F32)
        row1_sb = sb("row1_sb", [1, 128], F32)
        onesb_sb = sb("onesb_sb", [128, 1], BF16)
        s64_sb = sb("s64_sb", [64, 1], F32)
        act4_sb = sb("act4_sb", [1, NB], F32)
        lnc_sb = sb("lnc_sb", [128, 2], F32)

        x_sb = sb("x_sb", [128, NSUB * HID], F32)
        P1 = sb("P1", [128, NCH * T], BF16)
        P2 = sb("P2", [128, NCH * T], BF16)
        P3 = sb("P3", [128, NCH * T], BF16)
        stg = [sb(f"stg{i}", [128, HID], BF16) for i in range(4)]
        aoqh_sb = sb("aoqh", [128, 2 * 512], BF16)
        aoql_sb = sb("aoql", [128, 2 * 512], BF16)
        aofL = sb("aofL", [128, 4 * HID], F32)

        def aofv(s):
            return aofL[:, s * HID:(s + 1) * HID]
        escr = sb("escr", [128, 1024], F32)
        xn2f_sb = sb("xn2f", [128, HID], F32)
        gB_sb = sb("gB_sb", [128, 4 * 512], BF16)
        h_sc = gB_sb
        wq_buf = sb("wq_buf", [128, 2 * 4 * 512], BF16)

        sum1_sb = sb("sum1_sb", [128, 2 * NSUB], F32)
        sq_sb = sb("sq_sb", [128, 2 * NSUB], F32)
        st4_sb = sb("st4_sb", [128, 2 * NSUB * 4], F32)
        sum4_sb = sb("sum4_sb", [128, 8], F32)
        nrm_sb = sb("nrm_sb", [128, H * NB], F32)
        cc_sb = sb("cc_sb", [128, NSUB * H * NB], F32)
        ctmp_sb = sb("ctmp_sb", [128, 128], F32)
        scr64_sb = sb("scr64_sb", [128, 64], F32)
        wS_sb = sb("wS_sb", [128, 16], BF16)
        l3_sb = sb("l3_sb", [128, 24], F32)
        lg_sb = sb("lg_sb", [128, E8], F32)
        ze_sb = sb("ze_sb", [128, E8], F32)
        g_sb = sb("g_sb", [128, E8], BF16)
        sm_sb = sb("sm_sb", [128, 2 * NSUB * 4], F32)
        xstg_sb = sb("xstg", [128, HID], F32)
        xstg2_sb = sb("xstg2", [128, HID], F32)
        xrs = [sb(f"xrs{i}", [128, NCH * SUB], BF16) for i in range(2)]
        bnscr = aoql_sb[:].bitcast(F32)
        wqX = aofL[:].bitcast(BF16)

        def gT8v(c0, c1):
            return aoqh_sb[0:8, 512 + c0:512 + c1]

        pb = [ctx.enter_context(nc.psum_tensor(f"pb{i}", [128, 512], F32))
              for i in range(8)]

        class _Ch:
            def __init__(self, name):
                self.h = ctx.enter_context(nc.semaphore(name))
                self.n = 0

            def inc(self, step=1):
                self.n += step
                return self.n

        sd = _Ch("sd")
        sdc = _Ch("sdc")
        sg = _Ch("sg")
        st = _Ch("st")
        sv = _Ch("sv")
        ss = _Ch("ss")
        sp = _Ch("sp")

        _hwm = {}

        def W(q, ch, val):
            if not val:
                return
            key = (q, id(ch))
            if _hwm.get(key, -1) >= val:
                return
            _hwm[key] = val
            EMIT(q, lambda e, c=ch.h, v=val: e.wait_ge(c, v))

        class Rec:

            def __init__(self, q, ch, selfsync):
                self.q, self.ch, self.selfsync = q, ch, selfsync

            def w(self, ch, val):
                W(self.q, ch, val)

            def op(self, meth, *a, inc=None, **k):
                do_inc = self.selfsync if inc is None else inc
                p = self.ch.n if self.selfsync else 0
                v = self.ch.inc() if do_inc else self.ch.n
                def f(e, m=meth, a=a, k=k, c=self.ch.h, p=p, di=do_inc):
                    if p:
                        e.wait_ge(c, p)
                    ins = getattr(e, m)(*a, **k)
                    if di:
                        ins.then_inc(c, 1)
                EMIT(self.q, f)
                return v

        V = Rec("vector", sv, True)
        SC = Rec("scalar", ss, True)
        TE = Rec("tensor", st, False)
        G = Rec("gpsimd", sp, True)

        def DMA(q, out_ap, in_ap, **kw):
            ch = {"sync": sd, "scalar": sdc, "gpsimd": sg}[q]
            if _DMA_SERIAL:
                W(q, ch, ch.n)
            v = ch.inc(16)
            def f(e, o=out_ap, i=in_ap, c=ch.h, kw=kw):
                e.dma_start(o, i, **kw).then_inc(c, 16)
            EMIT(q, f)
            return v

        def TAP(name, src_ap, waits):
            if not _TAPS or name not in taps:
                return
            for ch, v in waits:
                W("sync", ch, v)
            tv = DMA("sync", taps[name], src_ap)
            taps.pop(name)
            for q in ("vector", "scalar", "tensor", "gpsimd"):
                W(q, sd, tv)

        def slab(buf, k):
            return buf[:, k * T:(k + 1) * T]

        def subv(buf, s):
            return buf[:, s * HID:(s + 1) * HID]

        def fwdT_out(buf, s):
            return buf[:].rearrange("p (c t) -> p c t", c=NCH)[:, :, s * SUB:(s + 1) * SUB]

        def revT_out(buf, gc):
            return buf[:].rearrange("p (s f) -> p s f", s=NSUB)[:, :, gc * SUB:(gc + 1) * SUB]

        def stat(s, j):
            return st4_sb[:, s * 4 + j:s * 4 + j + 1]

        def smc(s, j):
            return sm_sb[:, s * 4 + j:s * 4 + j + 1]

        wq_cast_tok = [DMA("gpsimd", wqb_d[Q], wqT_d[:, Q * 512:(Q + 1) * 512])
                       for Q in range(4)]

        tok_id = DMA("sync", id_sb[:], id_d)
        DMA("sync", e8_sb[:], e8_d)
        tok_ind = DMA("sync", ind_sb[:], ind_d)
        DMA("sync", br_sb[:], br_d)
        tok_bw = DMA("sync", bw_sb[:], bw_d)
        DMA("sync", row1_sb[:], on512_d)
        tok_ones = DMA("sync", onesb_sb[:], on128b_d)
        tok_lnc = DMA("sync", lnc_sb[:], lnc_d)
        W("scalar", sd, tok_lnc)
        Wrf_sb = xn2f_sb
        tok_wr = DMA("sync", Wrf_sb[:, 0:128].rearrange("p (k e) -> p k e", k=NCH),
                     wrT_d.rearrange("(k p) e -> p k e", p=128))

        cast_mb = DMA("gpsimd", mb_d, Mh_d)
        cast_ab = DMA("gpsimd", ab_d, AT_d.rearrange("(k p) j -> k p j", p=128))
        pre_cast_all = cast_ab

        tok_mn = DMA("sync", x_sb[:, 0:H * NB].rearrange("p (h n) -> p h n", h=H),
                     mn_d.rearrange("h p n -> p h n"))
        V.w(sd, tok_mn)
        cast_mn = V.op("tensor_copy", mn_sb[:], x_sb[:, 0:H * NB])
        W("sync", sv, cast_mn)
        tok_b = DMA("sync", x_sb[:, 0:4 * HID].rearrange("p (q j) -> p q j", q=4),
                    BT_d.rearrange("(q p) j -> p q j", p=128))
        V.w(sd, tok_b)
        cast_b = V.op("tensor_copy", B_sb[:], x_sb[:, 0:4 * HID])
        W("sync", sv, cast_b)
        xload_tok = {}
        for s in range(NSUB):
            xload_tok[(0, s)] = DMA(
                "sync", x_sb[:].rearrange("p (s f) -> p s f", s=NSUB)[:, s, :],
                x_d[s * SUB:(s + 1) * SUB, :])

        V.w(sd, tok_wr)
        V.op("tensor_copy", Wrh_sb[:], Wrf_sb[:, 0:128])
        V.op("tensor_sub", Wrr_sb[:], Wrf_sb[:, 0:128], Wrh_sb[:])
        for p in range(4):
            V.op("reduce_sum", nrm_sb[:, p:p + 1], B_sb[:, p * HID:(p + 1) * HID],
                 axis=AX.X)
        vec_bcs = V.op("tensor_copy", Bcs_sb[:], nrm_sb[:, 0:4])

        for c in range(NCH):
            W("sync", sv, sv.n)
            tok_ws = DMA("sync", xn2f_sb[:],
                         woT_d[c * 128:(c + 1) * 128, :])
            V.w(sd, tok_ws)
            V.op("reduce_sum", cc_sb[:, c:c + 1], xn2f_sb[:], axis=AX.X)
        vec_ws = V.op("tensor_copy", wS_sb[:], cc_sb[:, 0:16])


        TE.w(sd, tok_bw)
        TE.w(sd, tok_ones)
        mm_bwB = TE.op("matmul", pb[5][:, 0:64], row1_sb[0:1, 0:128],
                       bw_sb[0:1, :], start=True, stop=True, inc=True)
        TE.w(sv, cast_mn)
        mm_act1 = TE.op("matmul", pb[7][0:64, 0:1], mn_sb[:, 0:64], onesb_sb[:],
                        start=True, stop=True, inc=True)
        V.w(st, mm_act1)
        vec_s64 = V.op("tensor_copy", s64_sb[:], pb[7][0:64, 0:1])
        TE.w(sv, vec_s64)
        TE.w(sd, tok_ind)
        mm_act2 = TE.op("matmul", pb[6][0:1, 0:NB], s64_sb[:], ind_sb[:],
                        start=True, stop=True, inc=True)
        V.w(st, mm_act2)
        vec_act4 = V.op("tensor_single_scalar", act4_sb[0:1, :],
                        pb[6][0:1, 0:NB], EPS_A, ALU.is_ge)
        TE.w(sv, vec_act4)
        mm_actB = TE.op("matmul", pb[6][:, 0:NB], row1_sb[0:1, 0:128],
                        act4_sb[0:1, :], start=True, stop=True, inc=True)

        V.w(st, mm_bwB)
        vec_bwB = V.op("tensor_copy", wact_sb[:], pb[5][:, 0:64])
        w3i = wact_sb[:].rearrange("p (h n) -> p h n", n=NB)
        w3c = cc_sb[:, 0:64].rearrange("p (h n) -> p h n", n=NB)
        V.op("reduce_max", nrm_sb[:, 0:16].rearrange("p (h o) -> p h o", o=1),
             w3i, axis=AX.X)
        mbc = nrm_sb[:, 0:16].rearrange("p (h o) -> p h o", o=1) \
                             .to_broadcast([128, H, NB])
        vec_sm1 = V.op("tensor_tensor", w3c, w3i, mbc, ALU.subtract)
        SC.w(sv, vec_sm1)
        sc_sm = SC.op("activation", cc_sb[:, 0:64], cc_sb[:, 0:64], AF.Exp)
        V.w(ss, sc_sm)
        V.w(st, mm_actB)
        V.op("reduce_sum", nrm_sb[:, 16:32].rearrange("p (h o) -> p h o", o=1),
             w3c, axis=AX.X)
        V.op("reciprocal", nrm_sb[:, 16:32], nrm_sb[:, 16:32])
        zbc = nrm_sb[:, 16:32].rearrange("p (h o) -> p h o", o=1) \
                              .to_broadcast([128, H, NB])
        V.op("tensor_tensor", w3i, w3c, zbc, ALU.mult)
        V.op("tensor_copy", nrm_sb[:, 32:36], pb[6][:, 0:NB])
        abc = nrm_sb[:, 32:36].unsqueeze(1).to_broadcast([128, H, NB])
        vec_wactb = V.op("tensor_tensor", w3i, w3i, abc, ALU.mult)
        TAP("tap_wact", wact_sb[:], [(sv, vec_wactb)])

        p1_reader_tok = 0
        p2_reader_tok = 0
        p3_reader_tok = 0
        stg_free = [0, 0, 0, 0]
        bank_free = [(st, 0)] * 8
        bank_free[5] = (sv, vec_bwB)
        bank_free[6] = (sv, vec_wactb)
        bank_free[7] = (sv, vec_s64)
        aoq_tr_tok = 0
        prev_stream_tok = 0
        wo_cast_tok = []
        prev_x2_tok = 0
        cc_act_reader = [0]
        ln3_red_tok = {}
        ln3_red_prev = [0]
        ln2_x1_prev = [0]
        xn2f_red_prev = [0]
        xstg_guard = [(0, 0, 0), (0, 0, 0)]

        store_tok = {}
        tr_xn1 = {}

        def wait_bank(rec, b):
            ch, tok = bank_free[b]
            rec.w(ch, tok)

        def wslab(par, c0, c1):
            if par < 2:
                return wq_buf[:, par * 2048 + c0:par * 2048 + c1]
            return wqX[:, (par - 2) * 2048 + c0:(par - 2) * 2048 + c1]

        def group_load(src_d, cast_toks, g, prev_tok, nbuf=2):
            Q, kg = g // 4, g % 4
            par = g % nbuf
            W("scalar", st, prev_tok)
            if cast_toks:
                W("scalar", sg, cast_toks[Q])
            return DMA("scalar",
                       wslab(par, 0, 2048).rearrange("p (k j) -> p k j", k=4),
                       src_d[Q, kg * 512:(kg + 1) * 512, :]
                       .rearrange("(k p) j -> p k j", p=128))

        def stream_phase(src_d, cast_toks, rhs_buf, rhs_ready, evac_emit,
                         pre_ld=None, nbuf=2):
            nonlocal prev_stream_tok
            grp_done = {}
            ld_tok = dict(pre_ld or {})
            for Q in range(4):
                for kg in range(4):
                    g = Q * 4 + kg
                    par = g % nbuf
                    if g not in ld_tok:
                        ld_tok[g] = group_load(src_d, cast_toks, g,
                                               grp_done.get(g - nbuf,
                                                            prev_stream_tok),
                                               nbuf)
                    if g == 0:
                        TE.w(sd, rhs_ready)
                    if kg == 0:
                        for fc in range(4):
                            wait_bank(TE, (Q % 2) * 4 + fc)
                    TE.w(sdc, ld_tok[g])
                    for kl in range(4):
                        k = kg * 4 + kl
                        for fc in range(4):
                            last = (kl == 3 and fc == 3)
                            TE.op("matmul", pb[(Q % 2) * 4 + fc][:],
                                  wslab(par, kl * 512 + fc * 128,
                                        kl * 512 + (fc + 1) * 128),
                                  slab(rhs_buf, k),
                                  start=(k == 0), stop=(k == 15), inc=last)
                    grp_done[g] = st.n
                    if g + nbuf <= 15 and (g + nbuf) not in ld_tok:
                        ld_tok[g + nbuf] = group_load(src_d, cast_toks,
                                                      g + nbuf, grp_done[g],
                                                      nbuf)
                for fc in range(4):
                    b = (Q % 2) * 4 + fc
                    bank_free[b] = evac_emit(Q * 4 + fc, b, grp_done[Q * 4 + 3])
            prev_stream_tok = grp_done[15]
            return grp_done[15]


        def stat1(s, j):
            return st4_sb[:, 16 + s * 4 + j:16 + s * 4 + j + 1]

        def smc1(s, j):
            return sm_sb[:, 16 + s * 4 + j:16 + s * 4 + j + 1]

        def sum1c(mm, s):
            return sum1_sb[:, (mm % 2) * 4 + s:(mm % 2) * 4 + s + 1]

        def xstg_load(mm, s):
            buf = xstg_sb if s % 2 == 0 else xstg2_sb
            W("scalar", ss, xstg_guard[s % 2][0])
            W("scalar", sv, xstg_guard[s % 2][1])
            W("scalar", sp, xstg_guard[s % 2][2])
            return DMA("scalar", buf[:],
                       x_d[mm * T + s * SUB: mm * T + (s + 1) * SUB, :])

        def emit_ln1_sub(mm, s, xst):
            if mm == 0:
                xsrc = subv(x_sb, s)
                xch = sd
            else:
                xsrc = (xstg_sb if s % 2 == 0 else xstg2_sb)[:]
                xch = sdc
            V.w(xch, xst)
            V.w(sd, aoq_tr_tok)
            for c in range(4):
                V.op("bn_stats", bnscr[:, c * 6:(c + 1) * 6],
                     xsrc.rearrange("p (c f) -> p c f", c=4)[:, c, :])
            t_var = V.op("bn_aggr", st4_sb[:, 16 + s * 4:16 + s * 4 + 2],
                         bnscr[:, 0:24])
            ln1_red_tok[s] = t_var
            SC.w(xch, xst)
            SC.w(sv, t_var)
            SC.op("activation", stat1(s, 2), stat1(s, 1), AF.Ln,
                  bias=lnc_sb[:, 0:1])
            t_rstd = SC.op("activation", stat1(s, 2), stat1(s, 2), AF.Exp,
                           scale=-0.5)
            V.w(ss, t_rstd)
            t_negmr = V.op("tensor_scalar", stat1(s, 3), stat1(s, 2),
                           stat1(s, 0), -1.0, ALU.mult, ALU.mult)
            SC.w(sd, stg_free[s])
            SC.w(sv, t_negmr)
            t_ap = SC.op("activation", stg[s][:], xsrc, AF.Identity,
                         bias=stat1(s, 3), scale=stat1(s, 2))
            if mm > 0:
                xstg_guard[s % 2] = (t_ap, t_var, 0)
            return t_ap

        def emit_ln1_tr(mm, s, t_ap, p1_guard=0):
            W("sync", ss, t_ap)
            if mm == 0:
                if s == 0:
                    W("sync", sg, pre_cast_all)
            else:
                W("sync", sp, p1_guard)
            tok = DMA("sync", fwdT_out(P1, s), stg[s][:], transpose=True)
            stg_free[s] = tok
            return tok

        def emit_q(mm, xn1_trs, p2_guard, escr_guard=0, pre_ld=None,
                   mid=None):
            if p2_guard:
                W("gpsimd", st, p2_guard)
            elu_guard = {}
            mid = mid or {}

            def elu_evac(gc, b, mm_tok):
                if gc % 2 == 0:
                    scrE = escr[:, 0:512]
                    scrR = escr[:, 512:1024]
                else:
                    scrE = xstg_sb[:, 0:512]
                    scrR = xstg_sb[:, 512:1024]
                SC.w(st, mm_tok)
                if gc == 0:
                    SC.w(sv, escr_guard)
                SC.w(sp, elu_guard.get(gc - 2, 0))
                SC.op("activation", scrR, pb[b][:], AF.Relu, scale=-1.0)
                SC.op("activation", scrE, scrR, AF.Exp, scale=-1.0)
                t_e1 = SC.op("activation", scrR, pb[b][:], AF.Relu)
                G.w(ss, t_e1)
                t_e2 = G.op("tensor_tensor", slab(P2, gc), scrE, scrR,
                            ALU.add)
                elu_guard[gc] = t_e2
                if gc in mid:
                    mid[gc]()
                return (ss, t_e1)

            q_last = stream_phase(wqb_d, wq_cast_tok if mm == 0 else None,
                                  P1, xn1_trs[NSUB - 1], elu_evac,
                                  pre_ld=pre_ld, nbuf=4)
            if _TAPS and mm == 1:
                TAP("tap_P2", P2[:], [(sp, sp.n)])
            return q_last, sp.n

        p1_free_tok = [0]
        ln1_red_tok = {0: 0, 1: 0, 2: 0, 3: 0}
        m_rd_prev = {0: 0, 1: 0, 2: 0}
        a_rd_prev = {0: 0, 1: 0, 2: 0}
        tr0 = {}
        for s in range(NSUB):
            _ap0 = emit_ln1_sub(0, s, xload_tok[(0, s)])
            tr0[s] = emit_ln1_tr(0, s, _ap0)
        if True:
            W("gpsimd", sd, tr0[NSUB - 1])
            for Q in range(4):
                wo_cast_tok.append(
                    DMA("gpsimd", wob_d[Q], woT_d[:, Q * 512:(Q + 1) * 512]))
        q_last_mm, sig_done = emit_q(0, tr0, 0)

        for m in range(NMEG):
            p1_q_reader = q_last_mm

            comb_tr = {}
            wo_pre = {}
            t_cc4 = {}
            for s in range(NSUB):
                wait_bank(TE, 6)
                TE.w(sp, sig_done)
                for h in range(H):
                    TE.op("matmul", pb[6][:, h * 4:(h + 1) * 4],
                          P2[:, h * T + s * SUB: h * T + (s + 1) * SUB],
                          mn_sb[:, h * 4:(h + 1) * 4],
                          start=(h == 0), stop=(h == H - 1),
                          skip_group_check=True, inc=(h == H - 1))
                t_norm = st.n
                V.w(st, t_norm)
                V.w(sv, vec_wactb)
                V.w(ss, cc_act_reader[0])
                V.op("tensor_scalar_max", nrm_sb[:], pb[6][:, 0:64], EPS_A)
                V.op("reciprocal", nrm_sb[:], nrm_sb[:])
                t_cc4[s] = V.op("tensor_mul", cc_sb[:, s * 64:(s + 1) * 64],
                                nrm_sb[:], wact_sb[:])
                bank_free[6] = (sv, t_cc4[s])
            for s in range(NSUB):
                W("vector", sd, stg_free[s])
                W("scalar", sd, stg_free[s])
                W("gpsimd", sd, stg_free[s])
            m_rd_tok = {}
            comb_sv = {}
            comb_sp = {}
            for h in range(H):
                par = h % 3
                if m == 0:
                    W("sync", sg, cast_mb)
                W("sync", st, m_rd_tok.get(h - 3, m_rd_prev[par]))
                mld = DMA("sync", M_sb[:, par * 512:(par + 1) * 512], mb_d[h])
                TE.w(sd, mld)
                for s in range(NSUB):
                    bb = (h * 4 + s) % 3
                    wait_bank(TE, bb)
                    t_a = TE.op("matmul", pb[bb][:],
                                P2[:, h * T + s * SUB: h * T + (s + 1) * SUB],
                                M_sb[:, par * 512:(par + 1) * 512],
                                start=True, stop=True, inc=True)
                    if s == NSUB - 1:
                        m_rd_tok[h] = t_a
                        m_rd_prev[par] = t_a
                    ccol = lambda n, h=h, s=s: \
                        cc_sb[:, s * 64 + h * 4 + n:s * 64 + h * 4 + n + 1]
                    if s % 2 == 0:
                        V.w(st, t_a)
                        V.w(sv, t_cc4[s])
                        V.op("tensor_scalar", ctmp_sb[:], pb[bb][:, 0:128],
                             ccol(0), 0.0, ALU.mult, ALU.add)
                        V.op("scalar_tensor_tensor", ctmp_sb[:],
                             pb[bb][:, 128:256], ccol(1), ctmp_sb[:], ALU.mult,
                             ALU.add)
                        V.op("scalar_tensor_tensor", ctmp_sb[:],
                             pb[bb][:, 256:384], ccol(2), ctmp_sb[:], ALU.mult,
                             ALU.add)
                        t_comb = V.op("scalar_tensor_tensor",
                                      stg[s][:, h * 128:(h + 1) * 128],
                                      pb[bb][:, 384:512], ccol(3), ctmp_sb[:],
                                      ALU.mult, ALU.add)
                        bank_free[bb] = (sv, t_comb)
                        comb_sv[s] = t_comb
                    else:
                        so = (s // 2) * 512
                        SC.w(st, t_a)
                        SC.w(sv, t_cc4[s])
                        SC.w(sv, comb_sv.get(s, 0))
                        for n in range(4):
                            t_sc = SC.op("activation",
                                         escr[:, so + n * 128:so + (n + 1) * 128],
                                         pb[bb][:, n * 128:(n + 1) * 128],
                                         AF.Identity, scale=ccol(n))
                        bank_free[bb] = (ss, t_sc)
                        cc_act_reader[0] = t_sc
                        V.w(ss, t_sc)
                        e3 = escr[:, so:so + 512].rearrange(
                            "p (n f) -> p n f", n=4)
                        V.op("tensor_tensor", ctmp_sb[:],
                             e3[:, 0, :], e3[:, 2, :], ALU.add)
                        V.op("scalar_tensor_tensor", ctmp_sb[:],
                             e3[:, 1, :], 1.0, ctmp_sb[:], ALU.mult, ALU.add)
                        t_comb = V.op("scalar_tensor_tensor",
                                      stg[s][:, h * 128:(h + 1) * 128],
                                      e3[:, 3, :], 1.0, ctmp_sb[:],
                                      ALU.mult, ALU.add)
                        comb_sv[s] = t_comb
            for s in range(NSUB):
                if s == 0:
                    W("sync", st, p3_reader_tok)
                    if m == 0:
                        W("sync", sg, wo_cast_tok[-1])
                W("sync", sv, comb_sv[s])
                tok = DMA("sync", fwdT_out(P3, s), stg[s][:], transpose=True)
                comb_tr[s] = tok
                stg_free[s] = tok
                TE.w(sd, tok)
                if m == 0 and s == 0:
                    TE.w(sv, vec_ws)
                for k in range(NCH):
                    t_smm = TE.op("matmul", pb[6][:, 64 + s:65 + s],
                                  P3[:, k * T + s * SUB:k * T + (s + 1) * SUB],
                                  wS_sb[:, k:k + 1],
                                  start=(k == 0), stop=(k == NCH - 1),
                                  skip_group_check=True, inc=(k == NCH - 1))
                if s == NSUB - 1:
                    SC.w(st, t_smm)
                    SC.w(sv, ln2_x1_prev[0])
                    t_scp = SC.op("activation", sum4_sb[:, 0:4],
                                  pb[6][:, 64:68], AF.Identity)
                    bank_free[6] = (ss, t_scp)
                if s < 2:
                    wo_pre[s] = group_load(
                        wob_d, wo_cast_tok if m == 0 else None, s,
                        prev_stream_tok)
                if _TAPS and m == 1 and s == 0:
                    TAP("tap_comb", stg[0][:], [])
                    stg_free[0] = sd.n
                if _TAPS and m == 1 and s == 3:
                    TAP("tap_P3", P3[:], [])
            bank_last_mm = st.n

            xstg_tok = {}
            if m + 1 < NMEG:
                for s2 in range(2):
                    g0, g1, _ = xstg_guard[s2 % 2]
                    xstg_guard[s2 % 2] = (g0, g1, sig_done)
                    xstg_tok[s2] = xstg_load(m + 1, s2)

            revtr_last = [aoq_tr_tok]
            incr_g = {}
            incr_v = {}
            tr_pair = {}
            aofL3 = aofL[:].rearrange("p (s f) -> p s f", s=NSUB)
            P1r3 = P1[:].rearrange("p (s f) -> p s f", s=NSUB)
            P2r3 = P2[:].rearrange("p (s f) -> p s f", s=NSUB)

            def wo_evac(gc, b, mm_tok):
                fc2 = gc % 2
                SC.w(st, mm_tok)
                SC.w(sd, revtr_last[0])
                t_hi = SC.op("activation", aoqh_sb[:, fc2 * 512:(fc2 + 1) * 512],
                             pb[b][:], AF.Identity)
                V.w(ss, t_hi)
                V.w(sd, revtr_last[0])
                t_lo = V.op("tensor_sub", aoql_sb[:, fc2 * 512:(fc2 + 1) * 512],
                            pb[b][:], aoqh_sb[:, fc2 * 512:(fc2 + 1) * 512])
                if fc2 == 1:
                    W("sync", sv, t_lo)
                    W("sync", ss, t_hi)
                    if gc == 1:
                        W("sync", st, p1_q_reader)
                        W("sync", st, bank_last_mm)
                        if m == 0:
                            W("sync", sg, wo_cast_tok[-1])
                    tok = None
                    for c in (0, 1):
                        gcc = gc - 1 + c
                        DMA("sync", revT_out(P1, gcc),
                            aoqh_sb[:, c * 512:(c + 1) * 512], transpose=True)
                        tok = DMA("sync", revT_out(P2, gcc),
                                  aoql_sb[:, c * 512:(c + 1) * 512],
                                  transpose=True)
                    revtr_last[0] = tok
                    pi = gc // 2
                    tr_pair[pi] = tok
                    if pi >= 1:
                        emit_incr(pi - 1)
                return (sv, t_lo)

            def emit_incr(pi):
                for gcc in (2 * pi, 2 * pi + 1):
                    G.w(sd, tr_pair[pi])
                    if gcc == 0:
                        G.w(sv, ln3_red_prev[0])
                    t_g = G.op("tensor_tensor",
                               aofL3[:, :, gcc * SUB:(gcc + 1) * SUB],
                               P1r3[:, :, gcc * SUB:(gcc + 1) * SUB],
                               P2r3[:, :, gcc * SUB:(gcc + 1) * SUB],
                               ALU.add)
                    incr_g[gcc] = t_g
                    par = gcc % 2
                    SC.w(sp, t_g)
                    if gcc == 0:
                        SC.w(sp, sig_done)
                        SC.w(sv, comb_sv[1])
                        SC.w(sv, comb_sv[3])
                    if gcc - 2 in incr_v:
                        SC.w(sv, incr_v[gcc - 2])
                    sq3 = escr[:, par * 512:(par + 1) * 512].rearrange(
                        "p (s f) -> p s f", s=NSUB)
                    t_sq = SC.op("activation", sq3,
                                 aofL3[:, :, gcc * SUB:(gcc + 1) * SUB],
                                 AF.Square)
                    V.w(ss, t_sq)
                    t_rd = V.op("reduce_sum",
                                scr64_sb[:, gcc * 4:(gcc + 1) * 4]
                                .rearrange("p (s o) -> p s o", o=1),
                                sq3, axis=AX.X)
                    incr_v[gcc] = t_rd

            wo_last_mm = stream_phase(wob_d, wo_cast_tok if m == 0 else None,
                                      P3, comb_tr[NSUB - 1], wo_evac,
                                      pre_ld=wo_pre)
            aoq_tr_tok = revtr_last[0]
            q_pre = {}
            if m + 1 < NMEG:
                for gq in range(2):
                    q_pre[gq] = group_load(wqb_d, None, gq, prev_stream_tok)
            ln1_ap = {}
            if m + 1 < NMEG:
                for s in range(2):
                    ln1_ap[s] = emit_ln1_sub(m + 1, s, xstg_tok[s])
            emit_incr(7)
            recomb_last = incr_g[NCH - 1]
            incr_red_last = incr_v[NCH - 1]
            if _TAPS and m == 1:
                TAP("tap_ao", aofv(0), [(sp, recomb_last)])

            trn = {}
            if m + 1 < NMEG:
                for s in range(2):
                    trn[s] = emit_ln1_tr(m + 1, s, ln1_ap[s], recomb_last)
                    xstg_tok[s + 2] = xstg_load(m + 1, s + 2)
                for s in range(2, NSUB):
                    ap_s = emit_ln1_sub(m + 1, s, xstg_tok[s])
                    trn[s] = emit_ln1_tr(m + 1, s, ap_s, recomb_last)

            x1_tok = {}
            V.w(sp, recomb_last)
            V.w(ss, t_scp)
            V.op("tensor_add", scr64_sb[:, 0:32], scr64_sb[:, 0:32],
                 scr64_sb[:, 32:64])
            V.op("tensor_add", scr64_sb[:, 0:16], scr64_sb[:, 0:16],
                 scr64_sb[:, 16:32])
            V.op("tensor_add", scr64_sb[:, 0:8], scr64_sb[:, 0:8],
                 scr64_sb[:, 8:16])
            V.op("tensor_add", scr64_sb[:, 0:4], scr64_sb[:, 0:4],
                 scr64_sb[:, 4:8])
            for s in range(NSUB):
                V.op("tensor_single_scalar", stat(s, 0), sum4_sb[:, s:s + 1],
                     1.0 / HID, ALU.mult)
                V.op("tensor_scalar", stat(s, 1), scr64_sb[:, s:s + 1],
                     1.0 / HID, EPS_LN, ALU.mult, ALU.add)
                V.op("tensor_scalar", smc(s, 0), stat(s, 0), stat(s, 0), None,
                     ALU.mult)
                t_var2 = V.op("tensor_sub", stat(s, 1), stat(s, 1), smc(s, 0))
                SC.w(sv, t_var2)
                SC.op("activation", stat(s, 2), stat(s, 1), AF.Ln)
                t_rstd2 = SC.op("activation", stat(s, 2), stat(s, 2), AF.Exp,
                                scale=-0.5)
                V.w(ss, t_rstd2)
                V.op("tensor_scalar", stat(s, 3), stat(s, 2), stat(s, 0),
                     -1.0, ALU.mult, ALU.mult)
                V.w(sd if m == 0 else sdc, xload_tok[(m, s)])
                V.op("tensor_scalar", subv(x_sb, s), subv(x_sb, s),
                     stat(s, 3), None, ALU.add)
                t_stt2 = V.op("scalar_tensor_tensor", subv(x_sb, s),
                              aofv(s), stat(s, 2), subv(x_sb, s),
                              ALU.mult, ALU.add)
                x1_tok[s] = t_stt2
            ln2_x1_prev[0] = x1_tok[NSUB - 1]
            if _TAPS and m == 1:
                TAP("tap_x1", x_sb[:], [(sv, x1_tok[NSUB - 1])])

            xn2_tr = {}
            xres_tr = {}
            t_sqa3 = {}
            t_var3 = {}
            t_rstd3 = {}
            t_negmr3 = {}
            for s in range(NSUB):
                for c in range(4):
                    V.op("bn_stats",
                         bnscr[:, 96 + s * 24 + c * 6:96 + s * 24 + (c + 1) * 6],
                         subv(x_sb, s).rearrange("p (c f) -> p c f", c=4)[:, c, :])
                t_var3[s] = V.op("bn_aggr", sm_sb[:, 16 + s * 4:16 + s * 4 + 2],
                                 bnscr[:, 96 + s * 24:96 + (s + 1) * 24])
                ln3_red_tok[s] = t_var3[s]
            def ln3_fin(s):
                SC.w(sv, t_var3[s])
                SC.op("activation", stat(s, 2), sm_sb[:, 16 + s * 4 + 1:
                                                      16 + s * 4 + 2],
                      AF.Ln, bias=lnc_sb[:, 0:1])
                t_rstd3[s] = SC.op("activation", stat(s, 2), stat(s, 2),
                                   AF.Exp, scale=-0.5)
                V.w(ss, t_rstd3[s])
                t_negmr3[s] = V.op("tensor_scalar", stat(s, 3), stat(s, 2),
                                   sm_sb[:, 16 + s * 4:16 + s * 4 + 1],
                                   -1.0, ALU.mult, ALU.mult)
            t_apf3 = {}
            t_res3 = {}

            def ln3_apply(s):
                SC.w(sd, stg_free[s % 2])
                SC.w(sv, t_negmr3[s])
                SC.op("activation", stg[s % 2][:], subv(x_sb, s), AF.Identity,
                      bias=stat(s, 3), scale=stat(s, 2))
                if s >= 1:
                    SC.w(sv, t_res3.get(s - 1, 0))
                t_apf3[s] = SC.op("activation", xn2f_sb[:], subv(x_sb, s),
                                  AF.Identity, bias=stat(s, 3),
                                  scale=stat(s, 2))
                V.w(sd, stg_free[2 + s % 2])
                V.w(ss, t_apf3[s])
                t_res3[s] = V.op("tensor_sub", stg[2 + s % 2][:], xn2f_sb[:],
                                 stg[s % 2][:])
                W("sync", ss, t_apf3[s])
                if s == 0:
                    W("sync", st, wo_last_mm)
                tok1 = DMA("sync", fwdT_out(P3, s), stg[s % 2][:],
                           transpose=True)
                xn2_tr[s] = tok1
                stg_free[s % 2] = tok1
                if s < 2:
                    W("sync", sv, t_res3[s])
                    tok2 = DMA("sync",
                               xrs[s % 2][:].rearrange("p (c t) -> p c t", c=NCH),
                               stg[2 + s % 2][:], transpose=True)
                    xres_tr[s] = tok2
                    stg_free[2 + s % 2] = tok2
            ln3_red_prev[0] = x1_tok[NSUB - 1]

            if m + 1 < NMEG:
                W("scalar", sv, x1_tok[0])
                q_pre[2] = group_load(wqb_d, None, 2, prev_stream_tok, 4)
                q_pre[3] = group_load(wqb_d, None, 3, prev_stream_tok, 4)
                q_last_next, sig_next = emit_q(
                    m + 1, trn, bank_last_mm, incr_red_last, pre_ld=q_pre,
                    mid={3: lambda: ln3_fin(0), 4: lambda: ln3_fin(1),
                         5: lambda: ln3_fin(2), 6: lambda: ln3_fin(3),
                         7: lambda: ln3_apply(0), 9: lambda: ln3_apply(1),
                         11: lambda: ln3_apply(2), 13: lambda: ln3_apply(3)})
            else:
                for s in range(NSUB):
                    ln3_fin(s)
                    ln3_apply(s)
                q_last_next, sig_next = 0, 0

            g_tok = {}
            rt_tok = {}
            for s in range(NSUB):
                if s >= 2:
                    W("sync", sv, t_res3[s])
                    W("sync", st, rt_tok[s - 2])
                    tok2 = DMA("sync",
                               xrs[s % 2][:].rearrange("p (c t) -> p c t", c=NCH),
                               stg[2 + s % 2][:], transpose=True)
                    xres_tr[s] = tok2
                    stg_free[2 + s % 2] = tok2
                wait_bank(TE, 3)
                TE.w(sd, xn2_tr[s])
                TE.w(sd, xres_tr[s])
                first = True
                for k in range(NCH):
                    lhs = P3[:, k * T + s * SUB: k * T + (s + 1) * SUB]
                    TE.op("matmul", pb[3][:, 0:8], lhs,
                          Wrh_sb[:, k * 8:(k + 1) * 8],
                          start=first, stop=False, skip_group_check=True)
                    first = False
                    TE.op("matmul", pb[3][:, 8:16], lhs,
                          Wrr_sb[:, k * 8:(k + 1) * 8],
                          start=False, stop=False, skip_group_check=True)
                    lhr = xrs[s % 2][:, k * 128:(k + 1) * 128]
                    TE.op("matmul", pb[3][:, 16:24], lhr,
                          Wrh_sb[:, k * 8:(k + 1) * 8],
                          start=False, stop=False, skip_group_check=True)
                t_rt = TE.op("matmul", pb[3][:, 0:8], row1_sb[0:1, 0:128],
                             br_sb[0:1, :], start=False, stop=True,
                             skip_group_check=True, inc=True)
                rt_tok[s] = t_rt
                V.w(st, t_rt)
                V.op("tensor_copy", l3_sb[:], pb[3][:, 0:24])
                V.op("tensor_add", lg_sb[:], l3_sb[:, 0:8], l3_sb[:, 8:16])
                V.op("tensor_add", lg_sb[:], lg_sb[:], l3_sb[:, 16:24])
                V.op("reduce_max", smc(s, 0), lg_sb[:], axis=AX.X)
                t_rte = V.op("tensor_single_scalar", smc(s, 1), smc(s, 0),
                             -1.0, ALU.mult)
                bank_free[3] = (sv, t_rte)
                if _TAPS and m == 0 and s == 0:
                    TAP("tap_lg", l3_sb[:], [(sv, t_rte)])
                SC.w(sv, t_rte)
                t_rex = SC.op("activation", ze_sb[:], lg_sb[:], AF.Exp,
                              bias=smc(s, 1))
                V.w(ss, t_rex)
                V.op("reduce_sum", smc(s, 2), ze_sb[:], axis=AX.X)
                V.op("reciprocal", smc(s, 3), smc(s, 2))
                t_g = V.op("tensor_scalar", g_sb[:], lg_sb[:], smc(s, 0),
                           smc(s, 3), ALU.is_ge, ALU.mult)
                wait_bank(TE, 5)
                TE.w(sv, t_g)
                TE.w(sd, tok_id)
                t_gt = TE.op("transpose", pb[5][:].bitcast(BF16)[0:8, 0:128],
                             g_sb[:], id_sb[:], inc=True)
                V.w(st, t_gt)
                g_tok[s] = V.op("tensor_copy", gT8v(s * 128, (s + 1) * 128),
                                pb[5][:].bitcast(BF16)[0:8, 0:128])
                bank_free[5] = (sv, g_tok[s])
                if _TAPS and m == 0 and s == 0:
                    TAP("tap_g", g_sb[:], [(sv, g_tok[s])])

            for p in range(4):
                b = 4 + p
                wait_bank(TE, b)
                TE.w(sv, g_tok[NSUB - 1])
                t_gb = TE.op("matmul", pb[b][:], e8_sb[:, p * 128:(p + 1) * 128],
                             gT8v(0, 512), start=True, stop=True, inc=True)
                V.w(st, t_gb)
                t_gbe = V.op("tensor_copy", gB_sb[:, p * 512:(p + 1) * 512],
                             pb[b][:])
                bank_free[b] = (sv, t_gbe)
            if _TAPS:
                TAP("tap_gB", gB_sb[:], [(sv, sv.n)])

            h_tok = {}
            a_rd_tok = {}
            for k in range(NCH):
                par = k % 3
                if m == 0:
                    W("sync", sg, cast_ab)
                W("sync", st, a_rd_tok.get(k - 3, a_rd_prev[par]))
                ald = DMA("sync", A_sb[:, par * 512:(par + 1) * 512], ab_d[k])
                TE.w(sd, ald)
                if k == 0:
                    TE.w(sd, xn2_tr[NSUB - 1])
                    for p in range(4):
                        wait_bank(TE, p)
                for p in range(4):
                    TE.op("matmul", pb[p][:],
                          A_sb[:, par * 512 + p * 128:par * 512 + (p + 1) * 128],
                          slab(P3, k), start=(k == 0), stop=(k == NCH - 1),
                          inc=(p == 3))
                a_rd_tok[k] = st.n
                a_rd_prev[par] = st.n
            t_am = st.n
            for p in range(4):
                V.w(st, t_am)
                h_tok[p] = V.op("tensor_mul", h_sc[:, p * 512:(p + 1) * 512],
                                pb[p][:], gB_sb[:, p * 512:(p + 1) * 512])
                bank_free[p] = (sv, h_tok[p])
            if _TAPS:
                TAP("tap_h", h_sc[:], [(sv, h_tok[3])])

            for s in range(NSUB):
                for ob in range(4):
                    b = (s % 2) * 4 + ob
                    wait_bank(TE, b)
                    TE.w(sv, h_tok[3])
                    for p in range(4):
                        TE.op("matmul", pb[b][:],
                              h_sc[:, p * 512 + s * SUB: p * 512 + (s + 1) * SUB],
                              B_sb[:, p * HID + ob * 512: p * HID + (ob + 1) * 512],
                              start=(p == 0), stop=(p == 3), inc=(p == 3))
                t_ys = st.n
                V.w(st, t_ys)
                for ob in range(4):
                    V.op("bn_stats",
                         bnscr[:, 192 + s * 24 + ob * 6:192 + s * 24 + (ob + 1) * 6],
                         pb[(s % 2) * 4 + ob][:])
                t_var4 = V.op("bn_aggr", sm_sb[:, s * 4 + 2:s * 4 + 4],
                              bnscr[:, 192 + s * 24:192 + (s + 1) * 24])
                SC.w(sv, t_var4)
                SC.op("activation", stat(s, 2), smc(s, 3), AF.Ln,
                      bias=lnc_sb[:, 0:1], scale=lnc_sb[:, 1:2])
                t_rstd4 = SC.op("activation", stat(s, 2), stat(s, 2), AF.Exp,
                                scale=-0.5)
                V.w(ss, t_rstd4)
                V.op("tensor_scalar", stat(s, 3), stat(s, 2), smc(s, 2),
                     -SCALING, ALU.mult, ALU.mult)
                t_negmr4 = V.op("tensor_single_scalar", stat(s, 2), stat(s, 2),
                                SCALING, ALU.mult)
                V.op("tensor_scalar", subv(x_sb, s), subv(x_sb, s),
                     stat(s, 3), None, ALU.add)
                t_x2 = 0
                for ob in range(4):
                    t_x2 = V.op("scalar_tensor_tensor",
                                x_sb[:, s * HID + ob * 512:
                                     s * HID + (ob + 1) * 512],
                                pb[(s % 2) * 4 + ob][:], stat(s, 2),
                                x_sb[:, s * HID + ob * 512:
                                     s * HID + (ob + 1) * 512],
                                ALU.mult, ALU.add)
                for ob in range(4):
                    bank_free[(s % 2) * 4 + ob] = (sv, t_x2)
                prev_x2_tok = t_x2

                W("scalar", sv, t_x2)
                store_tok[(m, s)] = DMA(
                    "scalar", out_d[m * T + s * SUB: m * T + (s + 1) * SUB, :],
                    subv(x_sb, s))
                if m + 1 < NMEG:
                    xload_tok[(m + 1, s)] = DMA(
                        "scalar",
                        x_sb[:].rearrange("p (s f) -> p s f", s=NSUB)[:, s, :],
                        x_d[(m + 1) * T + s * SUB:(m + 1) * T + (s + 1) * SUB, :])

            q_last_mm = q_last_next
            sig_done = sig_next
            p2_reader_tok = bank_last_mm
            p3_reader_tok = st.n

        final_sd = sd.n
        EMIT("sync", lambda e, v=final_sd: e.wait_ge(sd.h, v))
        final_sdc = sdc.n
        EMIT("sync", lambda e, v=final_sdc: e.wait_ge(sdc.h, v))

        blk = ctx.enter_context(nc.Block())

        @blk.sync
        def _(e):
            for f in prog["sync"]:
                f(e)

        @blk.gpsimd
        def _(e):
            for f in prog["gpsimd"]:
                f(e)

        @blk.tensor
        def _(e):
            for f in prog["tensor"]:
                f(e)

        @blk.vector
        def _(e):
            for f in prog["vector"]:
                f(e)

        @blk.scalar
        def _(e):
            for f in prog["scalar"]:
                f(e)

    return nc


_NC = None


def kernel(**inputs):
    global _NC
    import ml_dtypes
    x = np.ascontiguousarray(
        np.asarray(inputs["L0_attn_hidden_states"], dtype=np.float32)
        .reshape(NCORE * ROWS, HID))
    wqT = np.ascontiguousarray(np.asarray(inputs["L0_attn_Wq"], np.float32).T)
    woT = np.ascontiguousarray(np.asarray(inputs["L0_attn_Wo"], np.float32).T)
    Mh = np.ascontiguousarray(
        np.asarray(inputs["L0_attn_memories"], np.float32).transpose(1, 2, 0, 3)
        .reshape(H, 128, NB * 128))
    mnT = np.ascontiguousarray(
        np.asarray(inputs["L0_attn_memory_norms"], np.float32).transpose(1, 2, 0))
    bw = np.ascontiguousarray(
        np.asarray(inputs["L0_attn_bank_weights"], np.float32).reshape(1, H * NB))
    AT = np.ascontiguousarray(
        np.asarray(inputs["L0_ffn_A"], np.float32).transpose(2, 0, 1)
        .reshape(HID, 512))
    BT = np.ascontiguousarray(
        np.asarray(inputs["L0_ffn_B"], np.float32).transpose(0, 2, 1)
        .reshape(512, HID))
    wrT = np.ascontiguousarray(np.asarray(inputs["L0_ffn_Wr"], np.float32).T)
    br = np.ascontiguousarray(np.asarray(inputs["L0_ffn_br"], np.float32)
                              .reshape(1, E8))
    import ml_dtypes as _mld
    ident = np.eye(128, dtype=_mld.bfloat16)
    e8sel = np.zeros((E8, 512), dtype=ml_dtypes.bfloat16)
    for p in range(4):
        e8sel[2 * p, p * 128: p * 128 + 64] = 1.0
        e8sel[2 * p + 1, p * 128 + 64: (p + 1) * 128] = 1.0
    ind64 = np.zeros((64, NB), dtype=np.float32)
    for h in range(H):
        for n in range(NB):
            ind64[h * NB + n, n] = 1.0
    ones512 = np.ones((1, 128), dtype=np.float32)
    ones128b = np.ones((128, 1), dtype=ml_dtypes.bfloat16)
    lnc = np.empty((128, 2), dtype=np.float32)
    lnc[:, 0] = EPS_LN
    lnc[:, 1] = SCALING * SCALING

    if _NC is None:
        _NC = build_nc()

    common = dict(wqT=wqT, woT=woT, Mh=Mh, mnT=mnT, bw=bw, AT=AT, BT=BT,
                  wrT=wrT, br=br, ident=ident, e8sel=e8sel, ind64=ind64,
                  ones512=ones512, ones128b=ones128b, lnc=lnc)
    in_maps = [dict(common, x=np.ascontiguousarray(x[c * ROWS:(c + 1) * ROWS]))
               for c in range(NCORE)]
    res = run_bass_kernel_spmd(_NC, in_maps, core_ids=list(range(NCORE)))
    out = np.concatenate([r["out"] for r in res.results], axis=0)
    return out.reshape(4, 4096, HID)
